# version 13
# baseline (speedup 1.0000x reference)
import os
import numpy as np
from contextlib import ExitStack
_KPARTS = set(os.environ.get("KPARTS", "all").split(","))
def _on(x):
    return "all" in _KPARTS or x in _KPARTS
import concourse.bass as bass
import concourse.mybir as mybir
from concourse.bass_utils import run_bass_kernel_spmd

F32 = mybir.dt.float32
BF16 = mybir.dt.bfloat16
I32 = mybir.dt.int32
AF = mybir.ActivationFunctionType
ALU = mybir.AluOpType
AX = mybir.AxisListType

ENGINES = ("sync", "tensor", "vector", "scalar", "gpsimd")

S, D, HID, NH, DK = 4096, 1024, 3072, 8, 128
NT = S // 128
NQST = 8
EPS = 1e-5
S127 = float(np.float32(1.0 / 127.0))


class _Op:
    __slots__ = ("eng", "fn", "dma", "idx", "needs_inc", "inc_value", "deps", "tag", "lane")

    def __init__(self, eng, fn, dma, idx, tag=""):
        self.eng, self.fn, self.dma, self.idx, self.tag = eng, fn, dma, idx, tag
        self.needs_inc = False
        self.inc_value = None
        self.deps = []
        self.lane = None


NLANES = 6


class Prog:
    def __init__(self, nc):
        self.nc = nc
        self.ops = {e: [] for e in ENGINES}
        self.last_w = {}
        self.readers = {}
        self.lane_rr = {}

    @staticmethod
    def _k(b):
        return b if isinstance(b, (str, tuple)) else id(b)

    def op(self, eng, fn, reads=(), writes=(), dma=False, tag=""):
        if dma:
            lane = self.lane_rr.get(eng, 0)
            self.lane_rr[eng] = (lane + 1) % NLANES
            writes = list(writes) + [("_lane", eng, lane)]
        o = _Op(eng, fn, dma, len(self.ops[eng]), tag)
        if dma:
            o.lane = (eng, lane)
        deps = {}

        def add(d):
            if d is None:
                return
            if d.eng == eng and not d.dma and eng == "tensor":
                return
            deps[(d.eng, d.idx)] = d

        for b in reads:
            add(self.last_w.get(self._k(b)))
        for b in writes:
            k = self._k(b)
            add(self.last_w.get(k))
            latest = {}
            for r in self.readers.get(k, ()):
                if r.dma:
                    add(r)
                elif r.idx >= latest.get(r.eng, (-1, None))[0]:
                    latest[r.eng] = (r.idx, r)
            for _, r in latest.values():
                add(r)
        o.deps = list(deps.values())
        self.ops[eng].append(o)
        for b in reads:
            self.readers.setdefault(self._k(b), []).append(o)
        for b in writes:
            k = self._k(b)
            self.last_w[k] = o
            self.readers[k] = []
        return o

    def barrier(self):
        lasts = []
        for e in ENGINES:
            if self.ops[e]:
                lasts.append(self.ops[e][-1])
                seen = set()
                for o in reversed(self.ops[e]):
                    if o.dma and o.lane not in seen:
                        seen.add(o.lane)
                        lasts.append(o)
        for e in ENGINES:
            o = _Op(e, (lambda E: E.nop()), False, len(self.ops[e]), "barrier")
            o.deps = [d for d in lasts if not (d.eng == e and not d.dma)]
            self.ops[e].append(o)

    def emit(self, block, sems):
        for e in ENGINES:
            for o in self.ops[e]:
                if o.dma:
                    o.needs_inc = True
                for d in o.deps:
                    d.needs_inc = True
        lane_c = {}
        for e in ENGINES:
            c = 0
            for o in self.ops[e]:
                if o.dma:
                    lc = lane_c.get(o.lane, 0) + 16
                    lane_c[o.lane] = lc
                    o.inc_value = lc
                elif o.needs_inc:
                    c += 1
                    o.inc_value = c

        def section(eng):
            def body(E):
                waited = {}
                for o in self.ops[eng]:
                    need = {}
                    for d in o.deps:
                        key = d.lane if d.dma else d.eng
                        if d.inc_value > need.get(key, 0):
                            need[key] = d.inc_value
                    for key, v in need.items():
                        if waited.get(key, 0) < v:
                            waited[key] = v
                            E.wait_ge(sems[key], v)
                    inst = o.fn(E)
                    if o.dma:
                        inst.then_inc(sems[o.lane], 16)
                    elif o.needs_inc:
                        inst.then_inc(sems[o.eng], 1)
            return body

        for e in ENGINES:
            if self.ops[e]:
                getattr(block, e)(section(e))


def build(stages=("A", "B", "C", "D")):
    nc = bass.Bass("TRN2", target_bir_lowering=False, dynamic_dma_scratch_size=16384)

    dx0 = nc.dram_tensor("x0", [S, D], F32, kind="ExternalInput")
    dpq = nc.dram_tensor("pq", [NH, D, DK], F32, kind="ExternalInput")
    dpk = nc.dram_tensor("pk", [D, DK], F32, kind="ExternalInput")
    dpv = nc.dram_tensor("pv", [D, DK], F32, kind="ExternalInput")
    dpo = nc.dram_tensor("po", [NH, D, DK], F32, kind="ExternalInput")
    dWk = nc.dram_tensor("Wk", [D, D], F32, kind="ExternalInput")
    dbk = nc.dram_tensor("bk", [D], F32, kind="ExternalInput")
    dWv = nc.dram_tensor("Wv", [D, D], F32, kind="ExternalInput")
    dbv = nc.dram_tensor("bv", [D], F32, kind="ExternalInput")
    dlng = nc.dram_tensor("lng", [D], F32, kind="ExternalInput")
    dlnb = nc.dram_tensor("lnb", [D], F32, kind="ExternalInput")
    dWo = nc.dram_tensor("Wo", [D, D], F32, kind="ExternalInput")
    dbo = nc.dram_tensor("bo", [D], F32, kind="ExternalInput")
    dwg = nc.dram_tensor("wg", [HID, D], I32, kind="ExternalInput")
    dwu = nc.dram_tensor("wu", [HID, D], I32, kind="ExternalInput")
    dwd = nc.dram_tensor("wd", [D, HID], I32, kind="ExternalInput")
    dy = nc.dram_tensor("y", [S, D], F32, kind="ExternalOutput")

    KDBG = int(os.environ.get("KDBG", "-1"))
    if KDBG >= 0:
        dbg_q = nc.dram_tensor("dbg_q", [128, NH * 512], BF16, kind="ExternalOutput")
        dbg_o = nc.dram_tensor("dbg_o", [128, NH * 512], BF16, kind="ExternalOutput")
        dbg_k = nc.dram_tensor("dbg_k", [128, S], BF16, kind="ExternalOutput")
        dbg_v = nc.dram_tensor("dbg_v", [128, NT * 130], BF16, kind="ExternalOutput")
        dbg_c = nc.dram_tensor("dbg_c", [128, 1], F32, kind="ExternalOutput")
        dbg_p = nc.dram_tensor("dbg_p", [128, 1024], BF16, kind="ExternalOutput")
        dbg_lt = nc.dram_tensor("dbg_lt", [128, D], BF16, kind="ExternalOutput")
        dbg_xt = nc.dram_tensor("dbg_xt", [128, D], F32, kind="ExternalOutput")
        dbg_st = nc.dram_tensor("dbg_st", [128, 3], F32, kind="ExternalOutput")
        dbg_lq = nc.dram_tensor("dbg_lq", [128, 8 * 512], BF16, kind="ExternalOutput")
    dx1 = nc.dram_tensor("x1s", [S, D], F32)
    dx2 = nc.dram_tensor("x2s", [S, D], F32)
    dx3 = nc.dram_tensor("x3s", [S, D], F32)
    dwgt = nc.dram_tensor("wgt_s", [128, 8 * HID], BF16)
    dwut = nc.dram_tensor("wut_s", [128, 8 * HID], BF16)
    dwdt = nc.dram_tensor("wdt_s", [128, 24 * D], BF16)

    p = Prog(nc)
    x_in = {"A": dx0, "B": dx1, "C": dx2, "D": dx3}
    x_out = {"A": dx1, "B": dx2, "C": dx3, "D": dy}
    x_out[stages[-1]] = dy
    KIN = {"A": "x0", "B": "x1", "C": "x2", "D": "x3"}
    KOUT = {"A": "x1", "B": "x2", "C": "x3", "D": "y"}
    KOUT[stages[-1]] = "y"

    with ExitStack() as g:
        T = lambda *a, **k: g.enter_context(nc.sbuf_tensor(*a, **k))
        consts = T("consts", [128, 8], F32)
        ones_col = T("ones_col", [128, 1], BF16)
        ones_row = T("ones_row", [1, 128], BF16)
        ident = T("ident", [128, 128], BF16)
        triu = T("triu", [128, 128], BF16)
        xb = [T(f"xb{i}", [128, D], F32) for i in range(5)]
        mu_all = T("mu_all", [128, NT], F32)
        rstd_all = T("rstd_all", [128, 2 * NT], F32)
        nmr_all = T("nmr_all", [128, 2 * NT], F32)
        bnbuf = T("bnbuf", [128, 12], F32)
        lnv = T("lnv", [128, 2], F32)
        ln_tm = [T(f"ln_tm{i}", [128, D], BF16) for i in range(1)]
        rpos = T("rpos", [128, NT], F32)
        smalls = T("smalls", [1, 8], F32)
        smallsb = T("smallsb", [1, 4], BF16)

        psA = g.enter_context(nc.psum_tensor("psA", [128, 1024], F32))
        psB = g.enter_context(nc.psum_tensor("psB", [128, 1024], F32))
        psC = g.enter_context(nc.psum_tensor("psC", [128, 512], F32))
        psD = g.enter_context(nc.psum_tensor("psD", [128, 512], F32))
        psE = g.enter_context(nc.psum_tensor("psE", [128, 512], F32))
        psT = g.enter_context(nc.psum_tensor("psT", [128, 1024], BF16))

        sems = {e: g.enter_context(nc.semaphore("m_" + e)) for e in ENGINES}
        for e in ("sync", "gpsimd"):
            for ln in range(6):
                sems[(e, ln)] = g.enter_context(nc.semaphore(f"d_{e}{ln}"))
        block = g.enter_context(nc.Block())

        cvals = {}

        def ccol(v):
            v = float(v)
            if v not in cvals:
                j = len(cvals)
                ap = consts[:, j:j + 1]
                p.op("gpsimd", lambda E, ap=ap, v=v: E.memset(ap, v),
                     writes=[("const", j)], tag="const")
                cvals[v] = (ap, ("const", j))
            return cvals[v][0]

        def ckey(v):
            return cvals[float(v)][1]

        ceps = ccol(EPS)

        p.op("gpsimd", lambda E: E.memset(ones_col[:, :], 1.0), writes=[ones_col])
        p.op("gpsimd", lambda E: E.memset(ones_row[:, :], 1.0), writes=[ones_row])
        p.op("gpsimd", lambda E: E.memset(ident[:, :], 1.0), writes=[ident])
        p.op("gpsimd", lambda E: E.affine_select(out=ident[:, :], in_=ident[:, :],
                                                 compare_op=ALU.is_equal, fill=0.0, base=0,
                                                 pattern=[[-1, 128]], channel_multiplier=1),
             reads=[ident], writes=[ident])
        p.op("gpsimd", lambda E: E.memset(triu[:, :], 0.0), writes=[triu])
        p.op("gpsimd", lambda E: E.affine_select(out=triu[:, :], in_=triu[:, :],
                                                 compare_op=ALU.is_gt, fill=1.0, base=0,
                                                 pattern=[[-1, 128]], channel_multiplier=1),
             reads=[triu], writes=[triu])
        posi = T("posi", [128, NT], I32)
        p.op("gpsimd", lambda E: E.iota(posi[:, :], pattern=[[128, NT]], base=1,
                                        channel_multiplier=1), writes=[posi])
        p.op("vector", lambda E: E.tensor_copy(rpos[:, :], posi[:, :]),
             reads=[posi], writes=[rpos])
        p.op("scalar", lambda E: E.activation(rpos[:, :], rpos[:, :], AF.Ln, bias=0.0, scale=1.0),
             reads=[rpos], writes=[rpos])
        p.op("scalar", lambda E: E.activation(rpos[:, :], rpos[:, :], AF.Exp, bias=0.0, scale=-0.5),
             reads=[rpos], writes=[rpos])

        def ln_stats(xtile, j, sb=None, sv=None):
            sb = bnbuf if sb is None else sb
            sv = lnv if sv is None else sv
            p.op("vector", lambda E, sb=sb: E.bn_stats(sb[:, 0:6], xtile[:, 0:512]),
                 reads=[xtile], writes=[sb], tag="bns0")
            p.op("vector", lambda E, sb=sb: E.bn_stats(sb[:, 6:12], xtile[:, 512:1024]),
                 reads=[xtile], writes=[sb], tag="bns1")
            p.op("vector", lambda E, sb=sb, sv=sv: E.bn_aggr(sv[:, :], sb[:, :]),
                 reads=[sb], writes=[sv], tag="bna")
            p.op("vector", lambda E, sv=sv: E.tensor_copy(mu_all[:, j:j + 1], sv[:, 0:1]),
                 reads=[sv], writes=[("mu", j)])
            p.op("scalar", lambda E, sv=sv: E.activation(rstd_all[:, j:j + 1], sv[:, 1:2],
                                                         AF.Ln, bias=ceps, scale=1.0),
                 reads=[sv, ckey(EPS)], writes=[("rstd", j)], tag="ln_rstd")
            p.op("scalar", lambda E: E.activation(rstd_all[:, j:j + 1], rstd_all[:, j:j + 1],
                                                  AF.Exp, bias=0.0, scale=-0.5),
                 reads=[("rstd", j)], writes=[("rstd", j)])
            p.op("vector", lambda E: E.scalar_tensor_tensor(
                nmr_all[:, j:j + 1], mu_all[:, j:j + 1], -1.0, rstd_all[:, j:j + 1],
                ALU.mult, ALU.mult),
                 reads=[("mu", j), ("rstd", j)], writes=[("nmr", j)], tag="nmr")

        def ln_apply(xtile, j, out):
            p.op("scalar", lambda E: E.activation(out[:, :], xtile[:, :], AF.Identity,
                                                  bias=nmr_all[:, j:j + 1],
                                                  scale=rstd_all[:, j:j + 1]),
                 reads=[xtile, ("rstd", j), ("nmr", j)], writes=[out], tag="ln_apply")

        def transpose8(src):
            for c in range(8):
                p.op("tensor", lambda E, c=c: E.transpose(psT[:, c * 128:(c + 1) * 128],
                                                          src[:, c * 128:(c + 1) * 128],
                                                          ident[:, :]),
                     reads=[src, ident], writes=[psT], tag="tr")

        fscope = ExitStack()
        preB = None
        if "B" in stages:
            wgT_g = fscope.enter_context(nc.sbuf_tensor("wgT_g", [128, 8 * HID], BF16))
            wuT_g = fscope.enter_context(nc.sbuf_tensor("wuT_g", [128, 8 * HID], BF16))
            wld_g = [fscope.enter_context(nc.sbuf_tensor(f"wld_g{i}", [128, 1024], BF16))
                     for i in range(2)]
            preB = (wgT_g, wuT_g, wld_g)

            def prep_wchunk(dsrc, dst, hc):
                w = wld_g[hc % 2]
                p.op("gpsimd", lambda E, w=w, dsrc=dsrc, hc=hc: E.dma_start(
                    w[:, 0:1024], dsrc[hc * 128:(hc + 1) * 128, :]),
                     writes=[w], dma=True, tag="ldw")
                for c in range(8):
                    p.op("tensor", lambda E, w=w, c=c: E.transpose(
                        psT[:, c * 128:(c + 1) * 128],
                        w[:, c * 128:(c + 1) * 128], ident[:, :]),
                         reads=[w, ident], writes=[psT], tag="wtr")
                p.op("vector", lambda E, dst=dst, hc=hc: E.tensor_copy(
                    bass.AP(dst, hc * 128, [[8 * HID, 128], [HID, 8], [1, 128]]),
                    psT[:, :]),
                     reads=[psT], writes=[dst], tag="wev")

        if "A" in stages:
            with ExitStack() as aw:
                TA = lambda *a: aw.enter_context(nc.sbuf_tensor(*a))
                K_fm = TA("K_fm", [128, S], BF16)
                Vt = TA("Vt", [128, NT * 130], BF16)
                pq_sb = TA("pq_sb", [128, NH * 8 * 128], BF16)
                pk_sb = TA("pk_sb", [128, 8 * 128], BF16)
                pv_sb = TA("pv_sb", [128, 8 * 128], BF16)
                poT = TA("poT", [128, NH * 1024], BF16)
                lnfm_q = TA("lnfm_q", [128, 8 * 512], BF16)
                Q_all = TA("Q_all", [128, NH * 512], BF16)
                PTb = [TA(f"PTb{i}", [128, 1024], BF16) for i in range(2)]
                Qsq = TA("Qsq", [128, 512], BF16)
                O_st = TA("O_st", [128, NH * 512], BF16)
                bcst = TA("bcst", [128, 512], F32)
                negc = TA("negc", [128, 1], F32)
                ksq_s = TA("ksq_s", [128, 128], BF16)
                mxk = TA("mxk", [1, 1], F32)
                ones2d = TA("ones2d", [128, 128], BF16)

                p.op("gpsimd", lambda E: E.dma_start(
                    pk_sb[:, :].rearrange("p (c k) -> p c k", c=8),
                    dpk.rearrange("(c p) k -> p c k", p=128)),
                     writes=[pk_sb], dma=True, tag="ld_pk")
                p.op("gpsimd", lambda E: E.dma_start(
                    pv_sb[:, :].rearrange("p (c k) -> p c k", c=8),
                    dpv.rearrange("(c p) k -> p c k", p=128)),
                     writes=[pv_sb], dma=True, tag="ld_pv")
                for hh in range(2):
                    p.op("gpsimd", lambda E, hh=hh: E.dma_start(
                        pq_sb[:, hh * 4096:(hh + 1) * 4096].rearrange(
                            "p (h c k) -> p h c k", h=4, c=8),
                        dpq[hh * 4:(hh + 1) * 4].rearrange("h (c p) k -> p h c k", p=128)),
                         writes=[pq_sb], dma=True, tag="ld_pq")

                p.op("gpsimd", lambda E: E.memset(
                    bass.AP(Vt, 128, [[NT * 130, 128], [130, NT], [1, 1]]), 1.0),
                     writes=[Vt], tag="vt_ones")

                p.op("vector", lambda E: E.memset(mxk[:, :], 0.0), writes=[mxk])
                p.op("gpsimd", lambda E: E.memset(ones2d[:, :], 1.0), writes=[ones2d])

                for t in range(NT):
                    xt = xb[t % 3]
                    p.op("sync", lambda E, t=t, xt=xt: E.dma_start(
                        xt[:, :], x_in["A"][t * 128:(t + 1) * 128, :]),
                         reads=[(KIN["A"], t)], writes=[xt], dma=True, tag="ldxA1")
                    ln_stats(xt, t)
                    lt = ln_tm[0]
                    ln_apply(xt, t, lt)
                    transpose8(lt)
                    lf = PTb[0]
                    p.op("vector", lambda E, lf=lf: E.tensor_copy(lf[:, :], psT[:, :]),
                         reads=[psT], writes=[lf], tag="evac_lnfm")
                    for c in range(8):
                        p.op("tensor", lambda E, c=c, lf=lf: E.matmul(
                            psC[:, 0:128], pk_sb[:, c * 128:(c + 1) * 128],
                            lf[:, c * 128:(c + 1) * 128],
                            start=(c == 0), stop=(c == 7)),
                             reads=[pk_sb, lf], writes=[psC], tag="Kmm")
                    p.op("vector", lambda E, t=t: E.tensor_copy(
                        K_fm[:, t * 128:(t + 1) * 128], psC[:, 0:128]),
                         reads=[psC], writes=[K_fm], tag="evacK")
                    for c in range(8):
                        p.op("tensor", lambda E, c=c, lf=lf: E.matmul(
                            psD[:, 0:128], lf[:, c * 128:(c + 1) * 128],
                            pv_sb[:, c * 128:(c + 1) * 128],
                            start=(c == 0), stop=(c == 7)),
                             reads=[pv_sb, lf], writes=[psD], tag="Vmm")
                    p.op("vector", lambda E, t=t: E.tensor_copy(
                        Vt[:, t * 130:t * 130 + 128], psD[:, 0:128]),
                         reads=[psD], writes=[Vt], tag="evacV")
                    p.op("vector", lambda E, t=t: E.tensor_tensor(
                        ksq_s[:, :], K_fm[:, t * 128:(t + 1) * 128],
                        K_fm[:, t * 128:(t + 1) * 128], ALU.mult),
                         reads=[K_fm], writes=[ksq_s], tag="ksq")
                    p.op("tensor", lambda E: E.matmul(psE[0:1, 0:128], ones_col[:, :],
                                                      ksq_s[:, :], start=True, stop=True),
                         reads=[ones_col, ksq_s], writes=[psE], tag="ksqmm")
                    p.op("vector", lambda E: E.tensor_reduce(smalls[:, 0:1], psE[0:1, 0:128],
                                                             AX.X, ALU.max),
                         reads=[psE], writes=[smalls], tag="ksqmax")
                    p.op("vector", lambda E: E.tensor_tensor(mxk[:, :], mxk[:, :],
                                                             smalls[:, 0:1], ALU.max),
                         reads=[mxk, smalls], writes=[mxk], tag="mxk")

                for hh in range(2):
                    p.op("gpsimd", lambda E, hh=hh: E.dma_start(
                        Q_all[:, :].rearrange("p (h c k) -> p h c k", h=4, c=8),
                        dpo[hh * 4:(hh + 1) * 4].rearrange("h (c p) k -> p h c k", p=128)),
                         writes=[Q_all], dma=True, tag="ld_po")
                    for h4 in range(4):
                        h = hh * 4 + h4
                        for c in range(8):
                            p.op("tensor", lambda E, h4=h4, c=c: E.transpose(
                                psT[:, c * 128:c * 128 + 128],
                                Q_all[:, (h4 * 8 + c) * 128:(h4 * 8 + c) * 128 + 128],
                                ident[:, :]),
                                 reads=[Q_all, ident], writes=[psT], tag="potr")
                        p.op("vector", lambda E, h=h: E.tensor_copy(
                            poT[:, h * 1024:(h + 1) * 1024], psT[:, :]),
                             reads=[psT], writes=[poT], tag="evac_poT")

                lnq_view = lnfm_q[:, :].rearrange("p (c n) -> p c n", c=8)
                for q in range(NQST):
                    for tt in range(4):
                        t = q * 4 + tt
                        xt = xb[t % 5]
                        p.op("sync", lambda E, t=t, xt=xt: E.dma_start(
                            xt[:, :], x_in["A"][t * 128:(t + 1) * 128, :]),
                             reads=[(KIN["A"], t)], writes=[xt], dma=True, tag="ldxA2")
                        lt = ln_tm[0]
                        ln_apply(xt, t, lt)
                        transpose8(lt)
                        p.op("vector", lambda E, tt=tt: E.tensor_copy(
                            bass.AP(lnfm_q, tt * 128, [[8 * 512, 128], [512, 8], [1, 128]]),
                            psT[:, :]),
                             reads=[psT], writes=[lnfm_q], tag="evac_lnfmq")
                        if KDBG == q and tt == 0:
                            p.op("sync", lambda E, lt=lt: E.dma_start(dbg_lt[:, :], lt[:, :]),
                                 reads=[lt], writes=[("dbglt",)], dma=True)
                            p.op("sync", lambda E, xt=xt: E.dma_start(dbg_xt[:, :], xt[:, :]),
                                 reads=[xt], writes=[("dbgxt",)], dma=True)
                            p.op("sync", lambda E, t=t: E.dma_start(dbg_st[:, 0:1], mu_all[:, t:t + 1], allow_slow_non_contiguous=True),
                                 reads=[mu_all], writes=[("dbgst",)], dma=True)
                            p.op("sync", lambda E, t=t: E.dma_start(dbg_st[:, 1:2], rstd_all[:, t:t + 1], allow_slow_non_contiguous=True),
                                 reads=[rstd_all], writes=[("dbgst",)], dma=True)
                            p.op("sync", lambda E, t=t: E.dma_start(dbg_st[:, 2:3], nmr_all[:, t:t + 1], allow_slow_non_contiguous=True),
                                 reads=[nmr_all], writes=[("dbgst",)], dma=True)
                    if KDBG == q:
                        p.op("sync", lambda E: E.dma_start(dbg_lq[:, :], lnfm_q[:, :]),
                             reads=[lnfm_q], writes=[("dbglq",)], dma=True)
                    p.op("vector", lambda E: E.memset(smalls[:, 1:2], 0.0), writes=[smalls])
                    for h in range(NH):
                        ps = psD if h % 2 == 0 else psE
                        for c in range(8):
                            p.op("tensor", lambda E, h=h, c=c, ps=ps: E.matmul(
                                ps[:, 0:512], pq_sb[:, (h * 8 + c) * 128:(h * 8 + c) * 128 + 128],
                                lnfm_q[:, c * 512:(c + 1) * 512],
                                start=(c == 0), stop=(c == 7)),
                                 reads=[pq_sb, lnfm_q], writes=[ps], tag="Qmm")
                        p.op("vector", lambda E, h=h, ps=ps: E.tensor_copy(
                            Q_all[:, h * 512:(h + 1) * 512], ps[:, 0:512]),
                             reads=[ps], writes=[Q_all], tag="evacQ")
                        p.op("vector", lambda E, h=h: E.tensor_tensor(
                            Qsq[:, :], Q_all[:, h * 512:(h + 1) * 512],
                            Q_all[:, h * 512:(h + 1) * 512], ALU.mult),
                             reads=[Q_all], writes=[Qsq], tag="qsq")
                        p.op("tensor", lambda E: E.matmul(psC[0:1, 0:512], ones_col[:, :],
                                                          Qsq[:, :], start=True, stop=True),
                             reads=[ones_col, Qsq], writes=[psC], tag="qsqmm")
                        p.op("vector", lambda E: E.tensor_reduce(
                            smalls[:, 2:3], psC[0:1, 0:512], AX.X, ALU.max),
                             reads=[psC], writes=[smalls], tag="qsqmax")
                        p.op("vector", lambda E: E.tensor_tensor(
                            smalls[:, 1:2], smalls[:, 1:2], smalls[:, 2:3], ALU.max),
                             reads=[smalls], writes=[smalls], tag="mxq")
                    p.op("vector", lambda E: E.tensor_tensor(
                        smalls[:, 3:4], smalls[:, 1:2], mxk[:, :], ALU.mult),
                         reads=[smalls, mxk], writes=[smalls], tag="csq")
                    p.op("scalar", lambda E: E.activation(smalls[:, 4:5], smalls[:, 3:4],
                                                          AF.Ln, bias=0.0, scale=1.0),
                         reads=[smalls], writes=[smalls])
                    p.op("scalar", lambda E: E.activation(smalls[:, 4:5], smalls[:, 4:5],
                                                          AF.Exp, bias=0.0, scale=0.5),
                         reads=[smalls], writes=[smalls])
                    p.op("vector", lambda E: E.tensor_scalar(
                        smallsb[:, 0:1], smalls[:, 4:5], -1.0, None, ALU.mult),
                         reads=[smalls], writes=[smallsb], tag="negc_bf")
                    p.op("tensor", lambda E: E.matmul(psC[:, 0:1], ones_row[:, :],
                                                      smallsb[:, 0:1], start=True, stop=True),
                         reads=[ones_row, smallsb], writes=[psC], tag="negc_bc")
                    p.op("vector", lambda E: E.tensor_copy(negc[:, :], psC[:, 0:1]),
                         reads=[psC], writes=[negc], tag="negc_cp")

                    def emit_qk(h, mp):
                        ps = psA if mp % 2 == 0 else psB
                        for half in range(2):
                            mt = 2 * mp + half
                            p.op("tensor", lambda E, ps=ps, mt=mt, half=half, h=h: E.matmul(
                                ps[:, half * 512:half * 512 + 512],
                                K_fm[:, mt * 128:(mt + 1) * 128],
                                Q_all[:, h * 512:(h + 1) * 512],
                                start=True, stop=True, skip_group_check=True),
                                 reads=[K_fm, Q_all], writes=[ps], tag="QK")

                    for h in range(NH):
                        if h == 0:
                            emit_qk(h, 0)
                        for mp in range(16):
                            ps = psA if mp % 2 == 0 else psB
                            pt = PTb[mp % 2]
                            if mp + 1 < 16:
                                emit_qk(h, mp + 1)
                            elif h + 1 < NH:
                                emit_qk(h + 1, 0)
                            for half in range(2):
                                p.op("scalar", lambda E, pt=pt, ps=ps, half=half: E.activation(
                                    pt[:, half * 512:half * 512 + 512],
                                    ps[:, half * 512:half * 512 + 512],
                                    AF.Exp, bias=negc[:, :], scale=1.0),
                                     reads=[ps, negc], writes=[pt], tag="exp")
                            for half in range(2):
                                mt = 2 * mp + half
                                p.op("tensor", lambda E, pt=pt, mt=mt, mp=mp, half=half: E.matmul(
                                    psC[:, 0:512], Vt[:, mt * 130:mt * 130 + 128],
                                    pt[:, half * 512:half * 512 + 512],
                                    start=(mp == 0 and half == 0),
                                    stop=(mp == 15 and half == 1), skip_group_check=True),
                                     reads=[Vt, pt], writes=[psC], tag="PV")
                                p.op("tensor", lambda E, pt=pt, mp=mp, half=half: E.matmul(
                                    psD[:, 0:512], ones2d[:, :],
                                    pt[:, half * 512:half * 512 + 512],
                                    start=(mp == 0 and half == 0),
                                    stop=(mp == 15 and half == 1), skip_group_check=True),
                                     reads=[ones2d, pt], writes=[psD], tag="smm")
                        p.op("vector", lambda E: E.reciprocal(bcst[:, :], psD[:, 0:512]),
                             reads=[psD], writes=[bcst], tag="recip")
                        p.op("vector", lambda E, h=h: E.tensor_tensor(
                            O_st[:, h * 512:(h + 1) * 512], psC[:, 0:512], bcst[:, :],
                            ALU.mult),
                             reads=[psC, bcst], writes=[O_st], tag="oscale")

                    if KDBG == q:
                        p.op("sync", lambda E: E.dma_start(dbg_q[:, :], Q_all[:, :]),
                             reads=[Q_all], writes=[("dbgq",)], dma=True)
                        p.op("sync", lambda E: E.dma_start(dbg_o[:, :], O_st[:, :]),
                             reads=[O_st], writes=[("dbgo",)], dma=True)
                        p.op("sync", lambda E: E.dma_start(dbg_k[:, :], K_fm[:, :]),
                             reads=[K_fm], writes=[("dbgk",)], dma=True)
                        p.op("sync", lambda E: E.dma_start(dbg_v[:, :], Vt[:, :]),
                             reads=[Vt], writes=[("dbgv",)], dma=True)
                        p.op("sync", lambda E: E.dma_start(dbg_c[:, :], negc[:, :]),
                             reads=[negc], writes=[("dbgc",)], dma=True)
                        p.op("sync", lambda E: E.dma_start(dbg_p[:, :], PTb[1][:, :]),
                             reads=[PTb[1]], writes=[("dbgp",)], dma=True)
                    for nsub in range(4):
                        t = q * 4 + nsub
                        xt = xb[t % 5]
                        psY = psA if nsub % 2 == 0 else psB
                        for dsub in range(2):
                            for h in range(NH):
                                p.op("tensor", lambda E, psY=psY, h=h, nsub=nsub, dsub=dsub: E.matmul(
                                    psY[:, dsub * 512:dsub * 512 + 512],
                                    O_st[:, h * 512 + nsub * 128:h * 512 + nsub * 128 + 128],
                                    poT[:, h * 1024 + dsub * 512:h * 1024 + dsub * 512 + 512],
                                    start=(h == 0), stop=(h == NH - 1), skip_group_check=True),
                                     reads=[O_st, poT], writes=[psY], tag="Ymm")
                        for dsub in range(2):
                            p.op("vector", lambda E, psY=psY, xt=xt, dsub=dsub: E.tensor_tensor(
                                xt[:, dsub * 512:dsub * 512 + 512],
                                psY[:, dsub * 512:dsub * 512 + 512],
                                xt[:, dsub * 512:dsub * 512 + 512], ALU.add),
                                 reads=[psY, xt], writes=[xt], tag="resid")
                        p.op("sync", lambda E, t=t, xt=xt: E.dma_start(
                            x_out["A"][t * 128:(t + 1) * 128, :], xt[:, :]),
                             reads=[xt], writes=[(KOUT["A"], t)], dma=True, tag="stxA")
                    if preB is not None:
                        for hc in range(q * 3, q * 3 + 3):
                            prep_wchunk(dwg, wgT_g, hc)
                            prep_wchunk(dwu, wuT_g, hc)
            p.barrier()

        def ffn_stage(st, pre=None):
            xin, xout = x_in[st], x_out[st]
            kin, kout = KIN[st], KOUT[st]
            with ExitStack() as bw:
                TB = lambda *a: bw.enter_context(nc.sbuf_tensor(*a))
                if pre is None:
                    wgT = TB(f"wgT{st}", [128, 8 * HID], BF16)
                    wuT = TB(f"wuT{st}", [128, 8 * HID], BF16)
                    wld = [TB(f"wld{st}{i}", [128, 1024], BF16) for i in range(2)]
                else:
                    wgT, wuT, wld = pre
                wdT = TB(f"wdT{st}", [128, 24 * D], BF16)
                lnq = TB(f"lnq{st}", [128, 8 * 512], BF16)
                hact = TB(f"hact{st}", [128, 24 * 512], BF16)
                gb = TB(f"gb{st}", [128, 512], BF16)
                ub = TB(f"ub{st}", [128, 512], BF16)

                for t in range(NT if _on("bstats") else 0):
                    xt = xb[t % 3]
                    p.op("sync", lambda E, t=t, xt=xt: E.dma_start(
                        xt[:, :], xin[t * 128:(t + 1) * 128, :]),
                         reads=[(kin, t)], writes=[xt], dma=True, tag="ldxS")
                    ln_stats(xt, t)

                if st == "B" and _on("bwprep"):
                    for dc in range(8):
                        for hg in range(3):
                            w = wld[(dc * 3 + hg) % 2]
                            p.op("gpsimd", lambda E, w=w, dc=dc, hg=hg: E.dma_start(
                                w[:, :], dwd[dc * 128:(dc + 1) * 128,
                                             hg * 1024:(hg + 1) * 1024]),
                                 writes=[w], dma=True, tag="ldwd")
                            for j in range(8):
                                p.op("tensor", lambda E, w=w, j=j: E.transpose(
                                    psT[:, j * 128:(j + 1) * 128],
                                    w[:, j * 128:(j + 1) * 128], ident[:, :]),
                                     reads=[w, ident], writes=[psT], tag="wdtr")
                            p.op("vector", lambda E, dc=dc, hg=hg: E.tensor_copy(
                                bass.AP(wdT, (hg * 8) * 1024 + dc * 128,
                                        [[24 * D, 128], [1024, 8], [1, 128]]),
                                psT[:, :]),
                                 reads=[psT], writes=[wdT], tag="wdev")
                elif st == "D":
                    p.op("sync", lambda E: E.dma_start(wgT[:, :], dwgt[:, :]),
                         reads=[("wgt_s",)], writes=[wgT], dma=True, tag="ldwgT")
                    p.op("sync", lambda E: E.dma_start(wuT[:, :], dwut[:, :]),
                         reads=[("wut_s",)], writes=[wuT], dma=True, tag="ldwuT")
                    p.op("sync", lambda E: E.dma_start(wdT[:, :], dwdt[:, :]),
                         reads=[("wdt_s",)], writes=[wdT], dma=True, tag="ldwdT")

                for qb in range(NQST if _on("bqloop") else 0):
                    for tt in range(4):
                        t = qb * 4 + tt
                        xt = xb[t % 5]
                        p.op("sync", lambda E, t=t, xt=xt: E.dma_start(
                            xt[:, :], xin[t * 128:(t + 1) * 128, :]),
                             reads=[(kin, t)], writes=[xt], dma=True, tag="ldxF")
                        lt = ln_tm[0]
                        ln_apply(xt, t, lt)
                        transpose8(lt)
                        p.op("vector", lambda E, tt=tt: E.tensor_copy(
                            bass.AP(lnq, tt * 128, [[8 * 512, 128], [512, 8], [1, 128]]),
                            psT[:, :]),
                             reads=[psT], writes=[lnq], tag="evac_lnq")
                    for hc in range(24):
                        ps = psA if hc % 2 == 0 else psB
                        for c in range(8):
                            p.op("tensor", lambda E, ps=ps, c=c, hc=hc: E.matmul(
                                ps[:, 0:512], wgT[:, c * HID + hc * 128:c * HID + hc * 128 + 128],
                                lnq[:, c * 512:(c + 1) * 512],
                                start=(c == 0), stop=(c == 7), skip_group_check=True),
                                 reads=[wgT, lnq], writes=[ps], tag="gmm")
                        for c in range(8):
                            p.op("tensor", lambda E, ps=ps, c=c, hc=hc: E.matmul(
                                ps[:, 512:1024], wuT[:, c * HID + hc * 128:c * HID + hc * 128 + 128],
                                lnq[:, c * 512:(c + 1) * 512],
                                start=(c == 0), stop=(c == 7), skip_group_check=True),
                                 reads=[wuT, lnq], writes=[ps], tag="umm")
                        p.op("scalar", lambda E, ps=ps: E.activation(
                            gb[:, :], ps[:, 0:512], AF.Silu, bias=0.0, scale=S127),
                             reads=[ps], writes=[gb], tag="silu")
                        p.op("scalar", lambda E, ps=ps: E.activation(
                            ub[:, :], ps[:, 512:1024], AF.Identity, bias=0.0, scale=S127),
                             reads=[ps], writes=[ub], tag="upev")
                        p.op("vector", lambda E, hc=hc: E.tensor_tensor(
                            hact[:, hc * 512:(hc + 1) * 512], gb[:, :], ub[:, :], ALU.mult),
                             reads=[gb, ub], writes=[hact], tag="hmul")
                    for nsub in range(4):
                        t = qb * 4 + nsub
                        xt = xb[t % 5]
                        psY = psA if nsub % 2 == 0 else psB
                        for dsub in range(2):
                            for hc in range(24):
                                p.op("tensor", lambda E, psY=psY, hc=hc, nsub=nsub, dsub=dsub: E.matmul(
                                    psY[:, dsub * 512:dsub * 512 + 512],
                                    hact[:, hc * 512 + nsub * 128:hc * 512 + nsub * 128 + 128],
                                    wdT[:, hc * 1024 + dsub * 512:hc * 1024 + dsub * 512 + 512],
                                    start=(hc == 0), stop=(hc == 23), skip_group_check=True),
                                     reads=[hact, wdT], writes=[psY], tag="dmm")
                        for dsub in range(2):
                            p.op("vector", lambda E, psY=psY, xt=xt, dsub=dsub: E.scalar_tensor_tensor(
                                xt[:, dsub * 512:dsub * 512 + 512],
                                psY[:, dsub * 512:dsub * 512 + 512], S127,
                                xt[:, dsub * 512:dsub * 512 + 512], ALU.mult, ALU.add),
                                 reads=[psY, xt], writes=[xt], tag="dresid")
                        p.op("sync", lambda E, t=t, xt=xt: E.dma_start(
                            xout[t * 128:(t + 1) * 128, :], xt[:, :]),
                             reads=[xt], writes=[(kout, t)], dma=True, tag="stxF")
                if st == "B":
                    p.op("sync", lambda E: E.dma_start(dwgt[:, :], wgT[:, :]),
                         reads=[wgT], writes=[("wgt_s",)], dma=True, tag="spillg")
                    p.op("sync", lambda E: E.dma_start(dwut[:, :], wuT[:, :]),
                         reads=[wuT], writes=[("wut_s",)], dma=True, tag="spillu")
                    p.op("sync", lambda E: E.dma_start(dwdt[:, :], wdT[:, :]),
                         reads=[wdT], writes=[("wdt_s",)], dma=True, tag="spilld")
            p.barrier()

        if "B" in stages:
            if "A" not in stages:
                for hc in range(24):
                    prep_wchunk(dwg, wgT_g, hc)
                    prep_wchunk(dwu, wuT_g, hc)
            ffn_stage("B", pre=preB)
            fscope.close()

        if "C" in stages:
            with ExitStack() as cw:
                TC = lambda *a: cw.enter_context(nc.sbuf_tensor(*a))
                WkT = TC("WkT", [128, 8 * 1024], BF16)
                WvT = TC("WvT", [128, 8 * 1024], BF16)
                WgoT = TC("WgoT", [128, 8 * 1024], BF16)

                QW = 4096
                keysb1 = TC("keysb", [128, QW], BF16)
                valsb2 = [TC(f"valsb{i}", [128, QW], BF16) for i in range(2)]
                knb2 = [TC(f"knb{i}", [128, QW], BF16) for i in range(2)]
                bb162 = [TC(f"bb16{i}", [128, QW], BF16) for i in range(2)]
                zb1 = TC("zb", [128, QW], BF16)
                m1_2 = [TC(f"m1_{i}", [128, 1024], BF16) for i in range(2)]
                m2_2 = [TC(f"m2_{i}", [128, 1024], BF16) for i in range(2)]
                m3_1 = TC("m3", [128, 1024], BF16)
                m1g2 = [TC(f"m1g{i}", [128, 1024], BF16) for i in range(2)]
                m2g2 = [TC(f"m2g{i}", [128, 1024], BF16) for i in range(2)]
                wldc = [m1g2[0], m1g2[1]]
                wperm = m2g2[0]
                lnC1 = TC("lnC", [128, 1024], BF16)
                bnbufC = [TC(f"bnbufC{i}", [128, 24], F32) for i in range(2)]
                lnvC = [TC(f"lnvC{i}", [128, 2], F32) for i in range(2)]
                crun = TC("crun", [1, 1024], F32)
                csnapAb = [TC(f"csnapAb{i}", [128, 1024], BF16) for i in range(2)]
                csnapBb = [TC(f"csnapBb{i}", [128, 1024], BF16) for i in range(2)]
                lfC2 = [TC(f"lfC{i}", [128, 1024], BF16) for i in range(2)]
                lfZ1 = TC("lfZ", [128, 1024], BF16)
                bkp = TC("bkp", [1, 1024], BF16)
                bvp = TC("bvp", [1, 1024], BF16)
                gpm = TC("gpm", [1, 1024], BF16)
                bpm = TC("bpm", [1, 1024], BF16)
                borow = TC("borow", [1, 1024], F32)
                biasCb = TC("biasCb", [1, 1024], BF16)
                gcol = TC("gcol", [128, 1], F32)
                onesC = TC("onesC", [128, 128], BF16)
                r1p = TC("r1p", [128, NT], F32)
                xbC = xb + [TC(f"xbC{i}", [128, D], F32) for i in range(3)]
                bcol = TC("bcol", [128, 1], BF16)

                def perm_row(dsrc, dst, kd):
                    p.op("sync", lambda E, dsrc=dsrc: E.dma_start(
                        crun[:, :], dsrc.rearrange("(a d) -> a d", a=1)),
                         writes=[crun], dma=True, tag="ldrow")
                    for c in range(4):
                        p.op("vector", lambda E, c=c, dst=dst: E.tensor_copy(
                            dst[:, c * 256:(c + 1) * 256],
                            bass.AP(crun, c, [[1024, 1], [4, 256]])),
                             reads=[crun], writes=[dst], tag="permrow")

                p.op("gpsimd", lambda E: E.memset(onesC[:, :], 1.0), writes=[onesC])
                perm_row(dbk, bkp, "bk")
                perm_row(dbv, bvp, "bv")
                perm_row(dlng, gpm, "lng")
                perm_row(dlnb, bpm, "lnb")
                p.op("sync", lambda E: E.dma_start(
                    borow[:, :], dbo.rearrange("(a d) -> a d", a=1)),
                     writes=[borow], dma=True, tag="ldbo")

                for nm, dsrc, dst in (("k", dWk, WkT), ("v", dWv, WvT)):
                    rsrc = dsrc.rearrange("(hh p four) i -> four hh p i", hh=2, p=128, four=4)
                    for q in range(8):
                        w = wldc[q % 2]
                        fo, hh = q // 2, q % 2
                        p.op("gpsimd", lambda E, w=w, rsrc=rsrc, fo=fo, hh=hh: E.dma_start(
                            w[:, :], rsrc[fo, hh, :, :]),
                             writes=[w], dma=True, tag="ldWkv")
                        for ci in range(8):
                            p.op("tensor", lambda E, w=w, ci=ci: E.transpose(
                                psT[:, ci * 128:(ci + 1) * 128],
                                w[:, ci * 128:(ci + 1) * 128], ident[:, :]),
                                 reads=[w, ident], writes=[psT], tag="wkvtr")
                        p.op("vector", lambda E, dst=dst, q=q: E.tensor_copy(
                            bass.AP(dst, q * 128, [[8 * 1024, 128], [1024, 8], [1, 128]]),
                            psT[:, :]),
                             reads=[psT], writes=[dst], tag="wkvev")

                for co in range(8):
                    w = wldc[co % 2]
                    p.op("gpsimd", lambda E, w=w, co=co: E.dma_start(
                        w[:, :], dWo[co * 128:(co + 1) * 128, :]),
                         writes=[w], dma=True, tag="ldWo")
                    for c in range(4):
                        p.op("vector", lambda E, w=w, c=c: E.tensor_copy(
                            wperm[:, c * 256:(c + 1) * 256],
                            bass.AP(w, c, [[1024, 128], [4, 256]])),
                             reads=[w], writes=[wperm], tag="woperm")
                    for ci in range(8):
                        p.op("tensor", lambda E, ci=ci: E.transpose(
                            psT[:, ci * 128:(ci + 1) * 128],
                            wperm[:, ci * 128:(ci + 1) * 128], ident[:, :]),
                             reads=[wperm, ident], writes=[psT], tag="wotr")
                    p.op("vector", lambda E, co=co: E.tensor_copy(
                        bass.AP(WgoT, co * 128, [[8 * 1024, 128], [1024, 8], [1, 128]]),
                        psT[:, :]),
                         reads=[psT], writes=[WgoT], tag="woev")

                for dsub in range(2):
                    for ci in range(8):
                        p.op("tensor", lambda E, ci=ci: E.matmul(
                            psD[:, 0:1], bpm[0:1, ci * 128:(ci + 1) * 128],
                            ones_row[0:1, 0:1], start=True, stop=True),
                             reads=[bpm, ones_row], writes=[psD], tag="bcolmm")
                        p.op("vector", lambda E: E.tensor_copy(bcol[:, :], psD[:, 0:1]),
                             reads=[psD], writes=[bcol], tag="bcolcp")
                        p.op("tensor", lambda E, ci=ci, dsub=dsub: E.matmul(
                            psE[0:1, 0:512], bcol[:, :],
                            WgoT[:, ci * 1024 + dsub * 512:ci * 1024 + dsub * 512 + 512],
                            start=(ci == 0), stop=(ci == 7), skip_group_check=True),
                             reads=[bcol, WgoT], writes=[psE], tag="biasmm")
                    p.op("vector", lambda E, dsub=dsub: E.tensor_tensor(
                        biasCb[0:1, dsub * 512:dsub * 512 + 512], psE[0:1, 0:512],
                        borow[0:1, dsub * 512:dsub * 512 + 512], ALU.add),
                         reads=[psE, borow], writes=[biasCb], tag="biasadd")

                for ci in range(8):
                    p.op("tensor", lambda E, ci=ci: E.matmul(
                        psD[:, 0:1], gpm[0:1, ci * 128:(ci + 1) * 128],
                        ones_row[0:1, 0:1], start=True, stop=True),
                         reads=[gpm, ones_row], writes=[psD], tag="gcolmm")
                    p.op("vector", lambda E: E.tensor_copy(gcol[:, :], psD[:, 0:1]),
                         reads=[psD], writes=[gcol], tag="gcolcp")
                    p.op("scalar", lambda E, ci=ci: E.activation(
                        WgoT[:, ci * 1024:(ci + 1) * 1024],
                        WgoT[:, ci * 1024:(ci + 1) * 1024],
                        AF.Identity, bias=0.0, scale=gcol[:, :]),
                         reads=[WgoT, gcol], writes=[WgoT], tag="gscale")


                def plane(buf, comp):
                    return bass.AP(buf, comp * 256, [[2048, 128], [1024, 2], [1, 256]])

                QUAT_BOUND = [
                    (0, [(1, 0, 0), (-1, 1, 1), (-1, 2, 2), (-1, 3, 3)]),
                    (1, [(1, 0, 1), (1, 1, 0), (1, 2, 3), (-1, 3, 2)]),
                    (2, [(1, 0, 2), (-1, 1, 3), (1, 2, 0), (1, 3, 1)]),
                    (3, [(1, 0, 3), (1, 1, 2), (-1, 2, 1), (1, 3, 0)]),
                ]
                QUAT_RETR = [
                    (0, [(1, 0, 0), (1, 1, 1), (1, 2, 2), (1, 3, 3)]),
                    (1, [(1, 1, 0), (-1, 0, 1), (-1, 2, 3), (1, 3, 2)]),
                    (2, [(1, 1, 3), (-1, 0, 2), (1, 2, 0), (-1, 3, 1)]),
                    (3, [(1, 2, 1), (-1, 0, 3), (-1, 1, 2), (1, 3, 0)]),
                ]

                def quat(table, a_buf, b_buf, out_buf, m1, m2, m1g, m2g):
                    for oc, terms in table:
                        eng = "vector" if oc < 3 else "gpsimd"
                        s1, s2 = (m1, m2) if oc < 3 else (m1g, m2g)
                        for i, (sg, ac, bc) in enumerate(terms):
                            tgt = s2 if i > 0 else s1
                            p.op(eng, lambda E, ac=ac, bc=bc, tgt=tgt,
                                 a_buf=a_buf, b_buf=b_buf: E.tensor_tensor(
                                     tgt[:, :], plane4(a_buf, ac), plane4(b_buf, bc),
                                     ALU.mult),
                                 reads=[a_buf, b_buf], writes=[tgt], tag="qmul")
                            if i > 0:
                                opx = ALU.add if sg > 0 else ALU.subtract
                                if i < 3:
                                    p.op(eng, lambda E, opx=opx, s1=s1, s2=s2: E.tensor_tensor(
                                        s1[:, :], s1[:, :], s2[:, :], opx),
                                         reads=[s1, s2], writes=[s1], tag="qacc")
                                else:
                                    p.op(eng, lambda E, opx=opx, oc=oc, s1=s1, s2=s2,
                                         out_buf=out_buf: E.tensor_tensor(
                                             plane4(out_buf, oc), s1[:, :], s2[:, :], opx),
                                         reads=[s1, s2], writes=[out_buf], tag="qout")

                p.op("vector", lambda E: E.memset(crun[:, :], 0.0), writes=[crun])
                p.op("vector", lambda E: E.memset(csnapAb[0][0:1, :], 0.0),
                     writes=[(id(csnapAb[0]), 0)])

                def cbrow(pb, u):
                    P = pb % 2
                    if u < 3:
                        return csnapAb[P], 32 * u
                    return csnapBb[P], 0

                def plane4(buf, comp):
                    return buf[:, comp * 1024:(comp + 1) * 1024]

                def tslice(buf, u, c0, nc_):
                    return bass.AP(buf, c0 * 1024 + u * 256,
                                   [[QW, 128], [1024, nc_], [1, 256]])

                def front(pb):
                    par = pb % 2
                    keysb, valsb, knb = keysb1, valsb2[par], knb2[par]
                    bb16 = bb162[par]
                    m1, m2, m3 = m1_2[par], m2_2[par], m3_1
                    m1g, m2g = m1g2[par], m2g2[par]
                    for u in range(4):
                        t = 4 * pb + u
                        xt = xbC[t % 8]
                        p.op("sync", lambda E, t=t, xt=xt: E.dma_start(
                            xt[:, :], x_in["C"][t * 128:(t + 1) * 128, :]),
                             reads=[(KIN["C"], t)], writes=[xt], dma=True, tag="ldxC")
                        ln_stats(xt, t)
                        lt = lnC1
                        ln_apply(xt, t, lt)
                        p.op("vector", lambda E, t=t: E.tensor_scalar(
                            r1p[:, t:t + 1], rstd_all[:, t:t + 1], 1.0, None, ALU.add),
                             reads=[("rstd", t)], writes=[("r1p", t)], tag="r1p")
                        transpose8(lt)
                        lf = lfC2[par]
                        p.op("scalar", lambda E, lf=lf: E.activation(
                            lf[:, :], psT[:, :], AF.Identity, bias=0.0, scale=1.0),
                             reads=[psT], writes=[lf], tag="evac_lnfmC")
                        for dst_ps, wt, brow, obuf in ((psA, WkT, bkp, keysb),
                                                       (psB, WvT, bvp, valsb)):
                            for ds in range(2):
                                for c in range(8):
                                    p.op("tensor", lambda E, dst_ps=dst_ps, wt=wt, c=c, ds=ds,
                                         lf=lf: E.matmul(
                                             dst_ps[:, ds * 512:ds * 512 + 512],
                                             lf[:, c * 128:(c + 1) * 128],
                                             wt[:, c * 1024 + ds * 512:c * 1024 + ds * 512 + 512],
                                             start=(c == 0), stop=False, skip_group_check=True),
                                         reads=[lf, wt], writes=[dst_ps], tag="kvmm")
                                p.op("tensor", lambda E, dst_ps=dst_ps, brow=brow, ds=ds: E.matmul(
                                    dst_ps[:, ds * 512:ds * 512 + 512], ones_row[:, :],
                                    brow[0:1, ds * 512:ds * 512 + 512],
                                    start=False, stop=True, skip_group_check=True),
                                     reads=[ones_row, brow], writes=[dst_ps], tag="kvbias")
                            p.op("scalar", lambda E, obuf=obuf, u=u, dst_ps=dst_ps: E.activation(
                                tslice(obuf, u, 0, 4), dst_ps[:, :],
                                AF.Identity, bias=0.0, scale=1.0),
                                 reads=[dst_ps], writes=[obuf], tag="kvev")
                    p.op("vector", lambda E, keysb=keysb, m3=m3: E.tensor_tensor(
                        m3[:, :], plane4(keysb, 0), plane4(keysb, 0), ALU.mult),
                         reads=[keysb], writes=[m3], tag="ss0")
                    for comp in range(1, 4):
                        p.op("vector", lambda E, keysb=keysb, m2=m2, comp=comp: E.tensor_tensor(
                            m2[:, :], plane4(keysb, comp), plane4(keysb, comp), ALU.mult),
                             reads=[keysb], writes=[m2], tag="ssq")
                        p.op("vector", lambda E, m2=m2, m3=m3: E.tensor_tensor(
                            m3[:, :], m3[:, :], m2[:, :], ALU.add),
                             reads=[m2, m3], writes=[m3], tag="ssacc")
                    p.op("vector", lambda E, m3=m3: E.tensor_scalar(
                        m3[:, :], m3[:, :], 1e-18, None, ALU.max),
                         reads=[m3], writes=[m3], tag="ssmax")
                    p.op("scalar", lambda E, m3=m3: E.activation(
                        m3[:, :], m3[:, :], AF.Ln, bias=0.0, scale=1.0),
                         reads=[m3], writes=[m3], tag="ssln")
                    p.op("scalar", lambda E, m3=m3: E.activation(
                        m3[:, :], m3[:, :], AF.Exp, bias=0.0, scale=-0.5),
                         reads=[m3], writes=[m3], tag="ssexp")
                    for comp in range(4):
                        p.op("vector", lambda E, comp=comp, m3=m3, knb=knb,
                             keysb=keysb: E.tensor_tensor(
                                 plane4(knb, comp), plane4(keysb, comp), m3[:, :], ALU.mult),
                             reads=[keysb, m3], writes=[knb], tag="knmul")
                    quat(QUAT_BOUND, valsb, knb, bb16, m1, m2, m1g, m2g)
                    for u in range(4):
                        for ds in range(2):
                            po = 0 if ds == 0 else 64
                            p.op("tensor", lambda E, u=u, ds=ds, po=po, bb16=bb16: E.matmul(
                                psE[po:po + 1, 0:512], ones_col[:, :],
                                tslice(bb16, u, 2 * ds, 2),
                                start=True, stop=True, skip_group_check=True),
                                 reads=[ones_col, bb16], writes=[psE], tag="Smm")
                        if u < 3:
                            db_, dbo_ = cbrow(pb, u + 1)
                        else:
                            db_, dbo_ = csnapAb[(pb + 1) % 2], 0
                        for ds in range(2):
                            po = 0 if ds == 0 else 64
                            p.op("vector", lambda E, ds=ds, po=po: E.tensor_tensor(
                                crun[0:1, ds * 512:ds * 512 + 512],
                                crun[0:1, ds * 512:ds * 512 + 512],
                                psE[po:po + 1, 0:512], ALU.add),
                                 reads=[crun, psE], writes=[crun], tag="carryupd")
                        p.op("vector", lambda E, db_=db_, dbo_=dbo_: E.tensor_copy(
                            db_[dbo_:dbo_ + 1, :], crun[:, :]),
                             reads=[crun], writes=[(id(db_), dbo_)], tag="carrybf")

                def back(pb):
                    par = pb % 2
                    valsb, knb = valsb2[par], knb2[par]
                    bb16, zb = bb162[par], zb1
                    m1, m2, m1g, m2g = m1_2[par], m2_2[par], m1g2[par], m2g2[par]
                    for u in range(4):
                        cb_, cbo = cbrow(pb, u)
                        for ds in range(2):
                            psM = psC if ds == 0 else psD
                            p.op("tensor", lambda E, psM=psM, u=u, ds=ds, bb16=bb16: E.matmul(
                                psM[:, 0:512], triu[:, :],
                                tslice(bb16, u, 2 * ds, 2),
                                start=True, stop=False, skip_group_check=True),
                                 reads=[triu, bb16], writes=[psM], tag="trilmm")
                            p.op("tensor", lambda E, psM=psM, ds=ds, cb_=cb_, cbo=cbo: E.matmul(
                                psM[:, 0:512], onesC[cbo:cbo + 1, :],
                                cb_[cbo:cbo + 1, ds * 512:ds * 512 + 512],
                                start=False, stop=True, skip_group_check=True),
                                 reads=[onesC, (id(cb_), cbo)], writes=[psM], tag="carrymm")
                        t = 4 * pb + u
                        for ds in range(2):
                            psM = psC if ds == 0 else psD
                            p.op("scalar", lambda E, psM=psM, u=u, ds=ds, t=t,
                                 valsb=valsb: E.activation(
                                     tslice(valsb, u, 2 * ds, 2),
                                     psM[:, 0:512], AF.Identity,
                                     bias=0.0, scale=rpos[:, t:t + 1]),
                                 reads=[psM, rpos], writes=[valsb], tag="Mevac")
                    quat(QUAT_RETR, valsb, knb, bb16, m1, m2, m1g, m2g)
                    retrb = bb16
                    for u in range(4):
                        t = 4 * pb + u
                        xt = xbC[t % 8]
                        sb, sv = bnbufC[par], lnvC[par]
                        for cc in range(4):
                            p.op("vector", lambda E, u=u, cc=cc, sb=sb, retrb=retrb: E.bn_stats(
                                sb[:, cc * 6:cc * 6 + 6],
                                retrb[:, cc * 1024 + u * 256:cc * 1024 + u * 256 + 256]),
                                 reads=[retrb], writes=[sb], tag="bnsr")
                        p.op("vector", lambda E, sb=sb, sv=sv: E.bn_aggr(sv[:, :], sb[:, :]),
                             reads=[sb], writes=[sv], tag="bnar")
                        j2 = NT + t
                        p.op("scalar", lambda E, j2=j2, sv=sv: E.activation(
                            rstd_all[:, j2:j2 + 1], sv[:, 1:2], AF.Ln, bias=ceps, scale=1.0),
                             reads=[sv, ckey(EPS)], writes=[("rstd", j2)], tag="rstdC1")
                        p.op("scalar", lambda E, j2=j2: E.activation(
                            rstd_all[:, j2:j2 + 1], rstd_all[:, j2:j2 + 1], AF.Exp,
                            bias=0.0, scale=-0.5),
                             reads=[("rstd", j2)], writes=[("rstd", j2)], tag="rstdC2")
                        p.op("vector", lambda E, j2=j2, sv=sv: E.scalar_tensor_tensor(
                            nmr_all[:, j2:j2 + 1], sv[:, 0:1], -1.0, rstd_all[:, j2:j2 + 1],
                            ALU.mult, ALU.mult),
                             reads=[sv, ("rstd", j2)], writes=[("nmr", j2)], tag="nmrC")
                        p.op("scalar", lambda E, u=u, j2=j2, zb=zb,
                             retrb=retrb: E.activation(
                                 zb[:, u * 1024:(u + 1) * 1024], tslice(retrb, u, 0, 4),
                                 AF.Identity,
                                 bias=nmr_all[:, j2:j2 + 1], scale=rstd_all[:, j2:j2 + 1]),
                             reads=[retrb, ("rstd", j2), ("nmr", j2)], writes=[zb], tag="zapply")
                        for c in range(8):
                            p.op("tensor", lambda E, c=c, u=u, zb=zb: E.transpose(
                                psT[:, c * 128:(c + 1) * 128],
                                zb[:, u * 1024 + c * 128:u * 1024 + c * 128 + 128],
                                ident[:, :]),
                                 reads=[zb, ident], writes=[psT], tag="ztr")
                        lf = lfZ1
                        p.op("scalar", lambda E, lf=lf: E.activation(
                            lf[:, :], psT[:, :], AF.Identity, bias=0.0, scale=1.0),
                             reads=[psT], writes=[lf], tag="zfmev")
                        for ds in range(2):
                            psO = psC if ds == 0 else psD
                            for c in range(8):
                                p.op("tensor", lambda E, psO=psO, c=c, ds=ds, lf=lf: E.matmul(
                                    psO[:, 0:512],
                                    lf[:, c * 128:(c + 1) * 128],
                                    WgoT[:, c * 1024 + ds * 512:c * 1024 + ds * 512 + 512],
                                    start=(c == 0), stop=False, skip_group_check=True),
                                     reads=[lf, WgoT], writes=[psO], tag="gomm")
                            p.op("tensor", lambda E, psO=psO, ds=ds: E.matmul(
                                psO[:, 0:512], ones_row[:, :],
                                biasCb[0:1, ds * 512:ds * 512 + 512],
                                start=False, stop=True, skip_group_check=True),
                                 reads=[ones_row, biasCb], writes=[psO], tag="gobias")
                        p.op("vector", lambda E, xt=xt, t=t: E.tensor_scalar(
                            xt[:, :], xt[:, :], r1p[:, t:t + 1], nmr_all[:, t:t + 1],
                            ALU.mult, ALU.add),
                             reads=[xt, ("r1p", t), ("nmr", t)], writes=[xt], tag="residC0")
                        for ds in range(2):
                            psO = psC if ds == 0 else psD
                            p.op("vector", lambda E, psO=psO, xt=xt, ds=ds: E.tensor_tensor(
                                xt[:, ds * 512:ds * 512 + 512],
                                psO[:, 0:512],
                                xt[:, ds * 512:ds * 512 + 512], ALU.add),
                                 reads=[psO, xt], writes=[xt], tag="residC1")
                        p.op("sync", lambda E, t=t, xt=xt: E.dma_start(
                            x_out["C"][t * 128:(t + 1) * 128, :], xt[:, :]),
                             reads=[xt], writes=[(KOUT["C"], t)], dma=True, tag="stxC")

                front(0)
                for pb in range(8):
                    if pb + 1 < 8:
                        front(pb + 1)
                    back(pb)
            p.barrier()

        if "D" in stages:
            ffn_stage("D")

        p.barrier()
        p.emit(block, sems)
    return nc


def _shard_inputs(inputs):
    per_core = []
    x = np.ascontiguousarray(inputs["L0_attn_a_x"])
    shared = {
        "pq": np.ascontiguousarray(inputs["L0_attn_a_proj_q"]),
        "pk": np.ascontiguousarray(inputs["L0_attn_a_proj_k"]),
        "pv": np.ascontiguousarray(inputs["L0_attn_a_proj_v"]),
        "po": np.ascontiguousarray(inputs["L0_attn_a_proj_o"]),
        "Wk": np.ascontiguousarray(inputs["L0_attn_b_Wk"]),
        "bk": np.ascontiguousarray(inputs["L0_attn_b_bk"]),
        "Wv": np.ascontiguousarray(inputs["L0_attn_b_Wv"]),
        "bv": np.ascontiguousarray(inputs["L0_attn_b_bv"]),
        "lng": np.ascontiguousarray(inputs["L0_attn_b_ln_g"]),
        "lnb": np.ascontiguousarray(inputs["L0_attn_b_ln_b"]),
        "Wo": np.ascontiguousarray(inputs["L0_attn_b_Wo"]),
        "bo": np.ascontiguousarray(inputs["L0_attn_b_bo"]),
        "wg": np.ascontiguousarray(inputs["L0_ffn_w_gate"]),
        "wu": np.ascontiguousarray(inputs["L0_ffn_w_up"]),
        "wd": np.ascontiguousarray(inputs["L0_ffn_w_down"]),
    }
    for b in range(8):
        m = {"x0": np.ascontiguousarray(x[b])}
        m.update(shared)
        per_core.append(m)
    return per_core


def kernel(**inputs):
    nc = build()
    in_maps = _shard_inputs(inputs)
    res = run_bass_kernel_spmd(nc, in_maps, core_ids=list(range(8)))
    return np.stack([r["y"] for r in res.results], axis=0)


# version 14
# speedup vs baseline: 1.0170x; 1.0170x over previous
import os
import numpy as np
from contextlib import ExitStack
_KPARTS = set(os.environ.get("KPARTS", "all").split(","))
def _on(x):
    return "all" in _KPARTS or x in _KPARTS
import concourse.bass as bass
import concourse.mybir as mybir
from concourse.bass_utils import run_bass_kernel_spmd

F32 = mybir.dt.float32
BF16 = mybir.dt.bfloat16
I32 = mybir.dt.int32
AF = mybir.ActivationFunctionType
ALU = mybir.AluOpType
AX = mybir.AxisListType

ENGINES = ("sync", "tensor", "vector", "scalar", "gpsimd")

S, D, HID, NH, DK = 4096, 1024, 3072, 8, 128
NT = S // 128
NQST = 8
EPS = 1e-5
S127 = float(np.float32(1.0 / 127.0))


class _Op:
    __slots__ = ("eng", "fn", "dma", "idx", "needs_inc", "inc_value", "deps", "tag", "lane")

    def __init__(self, eng, fn, dma, idx, tag=""):
        self.eng, self.fn, self.dma, self.idx, self.tag = eng, fn, dma, idx, tag
        self.needs_inc = False
        self.inc_value = None
        self.deps = []
        self.lane = None


NLANES = 6


class Prog:
    def __init__(self, nc):
        self.nc = nc
        self.ops = {e: [] for e in ENGINES}
        self.last_w = {}
        self.readers = {}
        self.lane_rr = {}

    @staticmethod
    def _k(b):
        return b if isinstance(b, (str, tuple)) else id(b)

    def op(self, eng, fn, reads=(), writes=(), dma=False, tag=""):
        if dma:
            lane = self.lane_rr.get(eng, 0)
            self.lane_rr[eng] = (lane + 1) % NLANES
            writes = list(writes) + [("_lane", eng, lane)]
        o = _Op(eng, fn, dma, len(self.ops[eng]), tag)
        if dma:
            o.lane = (eng, lane)
        deps = {}

        def add(d):
            if d is None:
                return
            if d.eng == eng and not d.dma and eng == "tensor":
                return
            deps[(d.eng, d.idx)] = d

        for b in reads:
            add(self.last_w.get(self._k(b)))
        for b in writes:
            k = self._k(b)
            add(self.last_w.get(k))
            latest = {}
            for r in self.readers.get(k, ()):
                if r.dma:
                    add(r)
                elif r.idx >= latest.get(r.eng, (-1, None))[0]:
                    latest[r.eng] = (r.idx, r)
            for _, r in latest.values():
                add(r)
        o.deps = list(deps.values())
        self.ops[eng].append(o)
        for b in reads:
            self.readers.setdefault(self._k(b), []).append(o)
        for b in writes:
            k = self._k(b)
            self.last_w[k] = o
            self.readers[k] = []
        return o

    def barrier(self):
        lasts = []
        for e in ENGINES:
            if self.ops[e]:
                lasts.append(self.ops[e][-1])
                seen = set()
                for o in reversed(self.ops[e]):
                    if o.dma and o.lane not in seen:
                        seen.add(o.lane)
                        lasts.append(o)
        for e in ENGINES:
            o = _Op(e, (lambda E: E.nop()), False, len(self.ops[e]), "barrier")
            o.deps = [d for d in lasts if not (d.eng == e and not d.dma)]
            self.ops[e].append(o)

    def emit(self, block, sems):
        for e in ENGINES:
            for o in self.ops[e]:
                if o.dma:
                    o.needs_inc = True
                for d in o.deps:
                    d.needs_inc = True
        lane_c = {}
        for e in ENGINES:
            c = 0
            for o in self.ops[e]:
                if o.dma:
                    lc = lane_c.get(o.lane, 0) + 16
                    lane_c[o.lane] = lc
                    o.inc_value = lc
                elif o.needs_inc:
                    c += 1
                    o.inc_value = c

        def section(eng):
            def body(E):
                waited = {}
                for o in self.ops[eng]:
                    need = {}
                    for d in o.deps:
                        key = d.lane if d.dma else d.eng
                        if d.inc_value > need.get(key, 0):
                            need[key] = d.inc_value
                    for key, v in need.items():
                        if waited.get(key, 0) < v:
                            waited[key] = v
                            E.wait_ge(sems[key], v)
                    inst = o.fn(E)
                    if o.dma:
                        inst.then_inc(sems[o.lane], 16)
                    elif o.needs_inc:
                        inst.then_inc(sems[o.eng], 1)
            return body

        for e in ENGINES:
            if self.ops[e]:
                getattr(block, e)(section(e))


def build(stages=("A", "B", "C", "D")):
    nc = bass.Bass("TRN2", target_bir_lowering=False, dynamic_dma_scratch_size=16384)

    dx0 = nc.dram_tensor("x0", [S, D], F32, kind="ExternalInput")
    dpq = nc.dram_tensor("pq", [NH, D, DK], F32, kind="ExternalInput")
    dpk = nc.dram_tensor("pk", [D, DK], F32, kind="ExternalInput")
    dpv = nc.dram_tensor("pv", [D, DK], F32, kind="ExternalInput")
    dpo = nc.dram_tensor("po", [NH, D, DK], F32, kind="ExternalInput")
    dWk = nc.dram_tensor("Wk", [D, D], F32, kind="ExternalInput")
    dbk = nc.dram_tensor("bk", [D], F32, kind="ExternalInput")
    dWv = nc.dram_tensor("Wv", [D, D], F32, kind="ExternalInput")
    dbv = nc.dram_tensor("bv", [D], F32, kind="ExternalInput")
    dlng = nc.dram_tensor("lng", [D], F32, kind="ExternalInput")
    dlnb = nc.dram_tensor("lnb", [D], F32, kind="ExternalInput")
    dWo = nc.dram_tensor("Wo", [D, D], F32, kind="ExternalInput")
    dbo = nc.dram_tensor("bo", [D], F32, kind="ExternalInput")
    dwg = nc.dram_tensor("wg", [HID, D], I32, kind="ExternalInput")
    dwu = nc.dram_tensor("wu", [HID, D], I32, kind="ExternalInput")
    dwd = nc.dram_tensor("wd", [D, HID], I32, kind="ExternalInput")
    dy = nc.dram_tensor("y", [S, D], F32, kind="ExternalOutput")

    KDBG = int(os.environ.get("KDBG", "-1"))
    if KDBG >= 0:
        dbg_q = nc.dram_tensor("dbg_q", [128, NH * 512], BF16, kind="ExternalOutput")
        dbg_o = nc.dram_tensor("dbg_o", [128, NH * 512], BF16, kind="ExternalOutput")
        dbg_k = nc.dram_tensor("dbg_k", [128, S], BF16, kind="ExternalOutput")
        dbg_v = nc.dram_tensor("dbg_v", [128, NT * 130], BF16, kind="ExternalOutput")
        dbg_c = nc.dram_tensor("dbg_c", [128, 1], F32, kind="ExternalOutput")
        dbg_p = nc.dram_tensor("dbg_p", [128, 1024], BF16, kind="ExternalOutput")
        dbg_lt = nc.dram_tensor("dbg_lt", [128, D], BF16, kind="ExternalOutput")
        dbg_xt = nc.dram_tensor("dbg_xt", [128, D], F32, kind="ExternalOutput")
        dbg_st = nc.dram_tensor("dbg_st", [128, 3], F32, kind="ExternalOutput")
        dbg_lq = nc.dram_tensor("dbg_lq", [128, 8 * 512], BF16, kind="ExternalOutput")
    dx1 = nc.dram_tensor("x1s", [S, D], F32)
    dx2 = nc.dram_tensor("x2s", [S, D], F32)
    dx3 = nc.dram_tensor("x3s", [S, D], F32)
    dwgt = nc.dram_tensor("wgt_s", [128, 8 * HID], BF16)
    dwut = nc.dram_tensor("wut_s", [128, 8 * HID], BF16)
    dwdt = nc.dram_tensor("wdt_s", [128, 24 * D], BF16)

    p = Prog(nc)
    x_in = {"A": dx0, "B": dx1, "C": dx2, "D": dx3}
    x_out = {"A": dx1, "B": dx2, "C": dx3, "D": dy}
    x_out[stages[-1]] = dy
    KIN = {"A": "x0", "B": "x1", "C": "x2", "D": "x3"}
    KOUT = {"A": "x1", "B": "x2", "C": "x3", "D": "y"}
    KOUT[stages[-1]] = "y"

    with ExitStack() as g:
        T = lambda *a, **k: g.enter_context(nc.sbuf_tensor(*a, **k))
        consts = T("consts", [128, 8], F32)
        ones_col = T("ones_col", [128, 1], BF16)
        ones_row = T("ones_row", [1, 128], BF16)
        ident = T("ident", [128, 128], BF16)
        triu = T("triu", [128, 128], BF16)
        xb = [T(f"xb{i}", [128, D], F32) for i in range(5)]
        mu_all = T("mu_all", [128, NT], F32)
        rstd_all = T("rstd_all", [128, 2 * NT], F32)
        nmr_all = T("nmr_all", [128, 2 * NT], F32)
        bnbuf = T("bnbuf", [128, 12], F32)
        lnv = T("lnv", [128, 2], F32)
        ln_tm = [T(f"ln_tm{i}", [128, D], BF16) for i in range(1)]
        rpos = T("rpos", [128, NT], F32)
        smalls = T("smalls", [1, 8], F32)
        smallsb = T("smallsb", [1, 4], BF16)

        psA = g.enter_context(nc.psum_tensor("psA", [128, 1024], F32))
        psB = g.enter_context(nc.psum_tensor("psB", [128, 1024], F32))
        psC = g.enter_context(nc.psum_tensor("psC", [128, 512], F32))
        psD = g.enter_context(nc.psum_tensor("psD", [128, 512], F32))
        psE = g.enter_context(nc.psum_tensor("psE", [128, 512], F32))
        psT = g.enter_context(nc.psum_tensor("psT", [128, 1024], BF16))

        sems = {e: g.enter_context(nc.semaphore("m_" + e)) for e in ENGINES}
        for e in ("sync", "gpsimd"):
            for ln in range(6):
                sems[(e, ln)] = g.enter_context(nc.semaphore(f"d_{e}{ln}"))
        block = g.enter_context(nc.Block())

        cvals = {}

        def ccol(v):
            v = float(v)
            if v not in cvals:
                j = len(cvals)
                ap = consts[:, j:j + 1]
                p.op("gpsimd", lambda E, ap=ap, v=v: E.memset(ap, v),
                     writes=[("const", j)], tag="const")
                cvals[v] = (ap, ("const", j))
            return cvals[v][0]

        def ckey(v):
            return cvals[float(v)][1]

        ceps = ccol(EPS)

        p.op("gpsimd", lambda E: E.memset(ones_col[:, :], 1.0), writes=[ones_col])
        p.op("gpsimd", lambda E: E.memset(ones_row[:, :], 1.0), writes=[ones_row])
        p.op("gpsimd", lambda E: E.memset(ident[:, :], 1.0), writes=[ident])
        p.op("gpsimd", lambda E: E.affine_select(out=ident[:, :], in_=ident[:, :],
                                                 compare_op=ALU.is_equal, fill=0.0, base=0,
                                                 pattern=[[-1, 128]], channel_multiplier=1),
             reads=[ident], writes=[ident])
        p.op("gpsimd", lambda E: E.memset(triu[:, :], 0.0), writes=[triu])
        p.op("gpsimd", lambda E: E.affine_select(out=triu[:, :], in_=triu[:, :],
                                                 compare_op=ALU.is_gt, fill=1.0, base=0,
                                                 pattern=[[-1, 128]], channel_multiplier=1),
             reads=[triu], writes=[triu])
        posi = T("posi", [128, NT], I32)
        p.op("gpsimd", lambda E: E.iota(posi[:, :], pattern=[[128, NT]], base=1,
                                        channel_multiplier=1), writes=[posi])
        p.op("vector", lambda E: E.tensor_copy(rpos[:, :], posi[:, :]),
             reads=[posi], writes=[rpos])
        p.op("scalar", lambda E: E.activation(rpos[:, :], rpos[:, :], AF.Ln, bias=0.0, scale=1.0),
             reads=[rpos], writes=[rpos])
        p.op("scalar", lambda E: E.activation(rpos[:, :], rpos[:, :], AF.Exp, bias=0.0, scale=-0.5),
             reads=[rpos], writes=[rpos])

        def ln_stats(xtile, j, sb=None, sv=None):
            sb = bnbuf if sb is None else sb
            sv = lnv if sv is None else sv
            p.op("vector", lambda E, sb=sb: E.bn_stats(sb[:, 0:6], xtile[:, 0:512]),
                 reads=[xtile], writes=[sb], tag="bns0")
            p.op("vector", lambda E, sb=sb: E.bn_stats(sb[:, 6:12], xtile[:, 512:1024]),
                 reads=[xtile], writes=[sb], tag="bns1")
            p.op("vector", lambda E, sb=sb, sv=sv: E.bn_aggr(sv[:, :], sb[:, :]),
                 reads=[sb], writes=[sv], tag="bna")
            p.op("vector", lambda E, sv=sv: E.tensor_copy(mu_all[:, j:j + 1], sv[:, 0:1]),
                 reads=[sv], writes=[("mu", j)])
            p.op("scalar", lambda E, sv=sv: E.activation(rstd_all[:, j:j + 1], sv[:, 1:2],
                                                         AF.Ln, bias=ceps, scale=1.0),
                 reads=[sv, ckey(EPS)], writes=[("rstd", j)], tag="ln_rstd")
            p.op("scalar", lambda E: E.activation(rstd_all[:, j:j + 1], rstd_all[:, j:j + 1],
                                                  AF.Exp, bias=0.0, scale=-0.5),
                 reads=[("rstd", j)], writes=[("rstd", j)])
            p.op("vector", lambda E: E.scalar_tensor_tensor(
                nmr_all[:, j:j + 1], mu_all[:, j:j + 1], -1.0, rstd_all[:, j:j + 1],
                ALU.mult, ALU.mult),
                 reads=[("mu", j), ("rstd", j)], writes=[("nmr", j)], tag="nmr")

        def ln_apply(xtile, j, out):
            p.op("scalar", lambda E: E.activation(out[:, :], xtile[:, :], AF.Identity,
                                                  bias=nmr_all[:, j:j + 1],
                                                  scale=rstd_all[:, j:j + 1]),
                 reads=[xtile, ("rstd", j), ("nmr", j)], writes=[out], tag="ln_apply")

        def transpose8(src):
            for c in range(8):
                p.op("tensor", lambda E, c=c: E.transpose(psT[:, c * 128:(c + 1) * 128],
                                                          src[:, c * 128:(c + 1) * 128],
                                                          ident[:, :]),
                     reads=[src, ident], writes=[psT], tag="tr")

        fscope = ExitStack()
        preB = None
        if "B" in stages:
            wgT_g = fscope.enter_context(nc.sbuf_tensor("wgT_g", [128, 8 * HID], BF16))
            wuT_g = fscope.enter_context(nc.sbuf_tensor("wuT_g", [128, 8 * HID], BF16))
            wld_g = [fscope.enter_context(nc.sbuf_tensor(f"wld_g{i}", [128, 1024], BF16))
                     for i in range(2)]
            preB = (wgT_g, wuT_g, wld_g)

            def prep_wchunk(dsrc, dst, hc):
                w = wld_g[hc % 2]
                p.op("gpsimd", lambda E, w=w, dsrc=dsrc, hc=hc: E.dma_start(
                    w[:, 0:1024], dsrc[hc * 128:(hc + 1) * 128, :]),
                     writes=[w], dma=True, tag="ldw")
                for c in range(8):
                    p.op("tensor", lambda E, w=w, c=c: E.transpose(
                        psT[:, c * 128:(c + 1) * 128],
                        w[:, c * 128:(c + 1) * 128], ident[:, :]),
                         reads=[w, ident], writes=[psT], tag="wtr")
                p.op("vector", lambda E, dst=dst, hc=hc: E.tensor_copy(
                    bass.AP(dst, hc * 128, [[8 * HID, 128], [HID, 8], [1, 128]]),
                    psT[:, :]),
                     reads=[psT], writes=[dst], tag="wev")

        if "A" in stages:
            with ExitStack() as aw:
                TA = lambda *a: aw.enter_context(nc.sbuf_tensor(*a))
                K_fm = TA("K_fm", [128, S], BF16)
                Vt = TA("Vt", [128, NT * 130], BF16)
                pq_sb = TA("pq_sb", [128, NH * 8 * 128], BF16)
                pk_sb = TA("pk_sb", [128, 8 * 128], BF16)
                pv_sb = TA("pv_sb", [128, 8 * 128], BF16)
                poT = TA("poT", [128, NH * 1024], BF16)
                lnfm_q = TA("lnfm_q", [128, 8 * 512], BF16)
                Q_all = TA("Q_all", [128, NH * 512], BF16)
                PTb = [TA(f"PTb{i}", [128, 1024], BF16) for i in range(2)]
                Qsq = TA("Qsq", [128, 512], BF16)
                O_st = TA("O_st", [128, NH * 512], BF16)
                bcst = TA("bcst", [128, 512], F32)
                negc = TA("negc", [128, 1], F32)
                ksq_s = TA("ksq_s", [128, 128], BF16)
                mxk = TA("mxk", [1, 1], F32)
                ones2d = TA("ones2d", [128, 128], BF16)

                p.op("gpsimd", lambda E: E.dma_start(
                    pk_sb[:, :].rearrange("p (c k) -> p c k", c=8),
                    dpk.rearrange("(c p) k -> p c k", p=128)),
                     writes=[pk_sb], dma=True, tag="ld_pk")
                p.op("gpsimd", lambda E: E.dma_start(
                    pv_sb[:, :].rearrange("p (c k) -> p c k", c=8),
                    dpv.rearrange("(c p) k -> p c k", p=128)),
                     writes=[pv_sb], dma=True, tag="ld_pv")
                for hh in range(2):
                    p.op("gpsimd", lambda E, hh=hh: E.dma_start(
                        pq_sb[:, hh * 4096:(hh + 1) * 4096].rearrange(
                            "p (h c k) -> p h c k", h=4, c=8),
                        dpq[hh * 4:(hh + 1) * 4].rearrange("h (c p) k -> p h c k", p=128)),
                         writes=[pq_sb], dma=True, tag="ld_pq")

                p.op("gpsimd", lambda E: E.memset(
                    bass.AP(Vt, 128, [[NT * 130, 128], [130, NT], [1, 1]]), 1.0),
                     writes=[Vt], tag="vt_ones")

                p.op("vector", lambda E: E.memset(mxk[:, :], 0.0), writes=[mxk])
                p.op("gpsimd", lambda E: E.memset(ones2d[:, :], 1.0), writes=[ones2d])

                for t in range(NT):
                    xt = xb[t % 3]
                    p.op("sync", lambda E, t=t, xt=xt: E.dma_start(
                        xt[:, :], x_in["A"][t * 128:(t + 1) * 128, :]),
                         reads=[(KIN["A"], t)], writes=[xt], dma=True, tag="ldxA1")
                    ln_stats(xt, t)
                    lt = ln_tm[0]
                    ln_apply(xt, t, lt)
                    transpose8(lt)
                    lf = PTb[0]
                    p.op("vector", lambda E, lf=lf: E.tensor_copy(lf[:, :], psT[:, :]),
                         reads=[psT], writes=[lf], tag="evac_lnfm")
                    for c in range(8):
                        p.op("tensor", lambda E, c=c, lf=lf: E.matmul(
                            psC[:, 0:128], pk_sb[:, c * 128:(c + 1) * 128],
                            lf[:, c * 128:(c + 1) * 128],
                            start=(c == 0), stop=(c == 7)),
                             reads=[pk_sb, lf], writes=[psC], tag="Kmm")
                    p.op("vector", lambda E, t=t: E.tensor_copy(
                        K_fm[:, t * 128:(t + 1) * 128], psC[:, 0:128]),
                         reads=[psC], writes=[K_fm], tag="evacK")
                    for c in range(8):
                        p.op("tensor", lambda E, c=c, lf=lf: E.matmul(
                            psD[:, 0:128], lf[:, c * 128:(c + 1) * 128],
                            pv_sb[:, c * 128:(c + 1) * 128],
                            start=(c == 0), stop=(c == 7)),
                             reads=[pv_sb, lf], writes=[psD], tag="Vmm")
                    p.op("vector", lambda E, t=t: E.tensor_copy(
                        Vt[:, t * 130:t * 130 + 128], psD[:, 0:128]),
                         reads=[psD], writes=[Vt], tag="evacV")
                    p.op("vector", lambda E, t=t: E.tensor_tensor(
                        ksq_s[:, :], K_fm[:, t * 128:(t + 1) * 128],
                        K_fm[:, t * 128:(t + 1) * 128], ALU.mult),
                         reads=[K_fm], writes=[ksq_s], tag="ksq")
                    p.op("tensor", lambda E: E.matmul(psE[0:1, 0:128], ones_col[:, :],
                                                      ksq_s[:, :], start=True, stop=True),
                         reads=[ones_col, ksq_s], writes=[psE], tag="ksqmm")
                    p.op("vector", lambda E: E.tensor_reduce(smalls[:, 0:1], psE[0:1, 0:128],
                                                             AX.X, ALU.max),
                         reads=[psE], writes=[smalls], tag="ksqmax")
                    p.op("vector", lambda E: E.tensor_tensor(mxk[:, :], mxk[:, :],
                                                             smalls[:, 0:1], ALU.max),
                         reads=[mxk, smalls], writes=[mxk], tag="mxk")

                for hh in range(2):
                    p.op("gpsimd", lambda E, hh=hh: E.dma_start(
                        Q_all[:, :].rearrange("p (h c k) -> p h c k", h=4, c=8),
                        dpo[hh * 4:(hh + 1) * 4].rearrange("h (c p) k -> p h c k", p=128)),
                         writes=[Q_all], dma=True, tag="ld_po")
                    for h4 in range(4):
                        h = hh * 4 + h4
                        for c in range(8):
                            p.op("tensor", lambda E, h4=h4, c=c: E.transpose(
                                psT[:, c * 128:c * 128 + 128],
                                Q_all[:, (h4 * 8 + c) * 128:(h4 * 8 + c) * 128 + 128],
                                ident[:, :]),
                                 reads=[Q_all, ident], writes=[psT], tag="potr")
                        p.op("vector", lambda E, h=h: E.tensor_copy(
                            poT[:, h * 1024:(h + 1) * 1024], psT[:, :]),
                             reads=[psT], writes=[poT], tag="evac_poT")

                lnq_view = lnfm_q[:, :].rearrange("p (c n) -> p c n", c=8)
                for q in range(NQST):
                    for tt in range(4):
                        t = q * 4 + tt
                        xt = xb[t % 5]
                        p.op("sync", lambda E, t=t, xt=xt: E.dma_start(
                            xt[:, :], x_in["A"][t * 128:(t + 1) * 128, :]),
                             reads=[(KIN["A"], t)], writes=[xt], dma=True, tag="ldxA2")
                        lt = ln_tm[0]
                        ln_apply(xt, t, lt)
                        transpose8(lt)
                        p.op("vector", lambda E, tt=tt: E.tensor_copy(
                            bass.AP(lnfm_q, tt * 128, [[8 * 512, 128], [512, 8], [1, 128]]),
                            psT[:, :]),
                             reads=[psT], writes=[lnfm_q], tag="evac_lnfmq")
                        if KDBG == q and tt == 0:
                            p.op("sync", lambda E, lt=lt: E.dma_start(dbg_lt[:, :], lt[:, :]),
                                 reads=[lt], writes=[("dbglt",)], dma=True)
                            p.op("sync", lambda E, xt=xt: E.dma_start(dbg_xt[:, :], xt[:, :]),
                                 reads=[xt], writes=[("dbgxt",)], dma=True)
                            p.op("sync", lambda E, t=t: E.dma_start(dbg_st[:, 0:1], mu_all[:, t:t + 1], allow_slow_non_contiguous=True),
                                 reads=[mu_all], writes=[("dbgst",)], dma=True)
                            p.op("sync", lambda E, t=t: E.dma_start(dbg_st[:, 1:2], rstd_all[:, t:t + 1], allow_slow_non_contiguous=True),
                                 reads=[rstd_all], writes=[("dbgst",)], dma=True)
                            p.op("sync", lambda E, t=t: E.dma_start(dbg_st[:, 2:3], nmr_all[:, t:t + 1], allow_slow_non_contiguous=True),
                                 reads=[nmr_all], writes=[("dbgst",)], dma=True)
                    if KDBG == q:
                        p.op("sync", lambda E: E.dma_start(dbg_lq[:, :], lnfm_q[:, :]),
                             reads=[lnfm_q], writes=[("dbglq",)], dma=True)
                    p.op("vector", lambda E: E.memset(smalls[:, 1:2], 0.0), writes=[smalls])
                    for h in range(NH):
                        ps = psD if h % 2 == 0 else psE
                        for c in range(8):
                            p.op("tensor", lambda E, h=h, c=c, ps=ps: E.matmul(
                                ps[:, 0:512], pq_sb[:, (h * 8 + c) * 128:(h * 8 + c) * 128 + 128],
                                lnfm_q[:, c * 512:(c + 1) * 512],
                                start=(c == 0), stop=(c == 7)),
                                 reads=[pq_sb, lnfm_q], writes=[ps], tag="Qmm")
                        p.op("vector", lambda E, h=h, ps=ps: E.tensor_copy(
                            Q_all[:, h * 512:(h + 1) * 512], ps[:, 0:512]),
                             reads=[ps], writes=[Q_all], tag="evacQ")
                        p.op("vector", lambda E, h=h: E.tensor_tensor(
                            Qsq[:, :], Q_all[:, h * 512:(h + 1) * 512],
                            Q_all[:, h * 512:(h + 1) * 512], ALU.mult),
                             reads=[Q_all], writes=[Qsq], tag="qsq")
                        p.op("tensor", lambda E: E.matmul(psC[0:1, 0:512], ones_col[:, :],
                                                          Qsq[:, :], start=True, stop=True),
                             reads=[ones_col, Qsq], writes=[psC], tag="qsqmm")
                        p.op("vector", lambda E: E.tensor_reduce(
                            smalls[:, 2:3], psC[0:1, 0:512], AX.X, ALU.max),
                             reads=[psC], writes=[smalls], tag="qsqmax")
                        p.op("vector", lambda E: E.tensor_tensor(
                            smalls[:, 1:2], smalls[:, 1:2], smalls[:, 2:3], ALU.max),
                             reads=[smalls], writes=[smalls], tag="mxq")
                    p.op("vector", lambda E: E.tensor_tensor(
                        smalls[:, 3:4], smalls[:, 1:2], mxk[:, :], ALU.mult),
                         reads=[smalls, mxk], writes=[smalls], tag="csq")
                    p.op("scalar", lambda E: E.activation(smalls[:, 4:5], smalls[:, 3:4],
                                                          AF.Ln, bias=0.0, scale=1.0),
                         reads=[smalls], writes=[smalls])
                    p.op("scalar", lambda E: E.activation(smalls[:, 4:5], smalls[:, 4:5],
                                                          AF.Exp, bias=0.0, scale=0.5),
                         reads=[smalls], writes=[smalls])
                    p.op("vector", lambda E: E.tensor_scalar(
                        smallsb[:, 0:1], smalls[:, 4:5], -1.0, None, ALU.mult),
                         reads=[smalls], writes=[smallsb], tag="negc_bf")
                    p.op("tensor", lambda E: E.matmul(psC[:, 0:1], ones_row[:, :],
                                                      smallsb[:, 0:1], start=True, stop=True),
                         reads=[ones_row, smallsb], writes=[psC], tag="negc_bc")
                    p.op("vector", lambda E: E.tensor_copy(negc[:, :], psC[:, 0:1]),
                         reads=[psC], writes=[negc], tag="negc_cp")

                    def emit_qk(h, mp):
                        ps = psA if mp % 2 == 0 else psB
                        for half in range(2):
                            mt = 2 * mp + half
                            p.op("tensor", lambda E, ps=ps, mt=mt, half=half, h=h: E.matmul(
                                ps[:, half * 512:half * 512 + 512],
                                K_fm[:, mt * 128:(mt + 1) * 128],
                                Q_all[:, h * 512:(h + 1) * 512],
                                start=True, stop=True, skip_group_check=True),
                                 reads=[K_fm, Q_all], writes=[ps], tag="QK")

                    for h in range(NH):
                        if h == 0:
                            emit_qk(h, 0)
                        for mp in range(16):
                            ps = psA if mp % 2 == 0 else psB
                            pt = PTb[mp % 2]
                            if mp + 1 < 16:
                                emit_qk(h, mp + 1)
                            elif h + 1 < NH:
                                emit_qk(h + 1, 0)
                            p.op("scalar", lambda E, pt=pt, ps=ps: E.activation(
                                pt[:, :], ps[:, :], AF.Exp, bias=negc[:, :], scale=1.0),
                                 reads=[ps, negc], writes=[pt], tag="exp")
                            for half in range(2):
                                mt = 2 * mp + half
                                p.op("tensor", lambda E, pt=pt, mt=mt, mp=mp, half=half: E.matmul(
                                    psC[:, 0:512], Vt[:, mt * 130:mt * 130 + 128],
                                    pt[:, half * 512:half * 512 + 512],
                                    start=(mp == 0 and half == 0),
                                    stop=(mp == 15 and half == 1), skip_group_check=True),
                                     reads=[Vt, pt], writes=[psC], tag="PV")
                                p.op("tensor", lambda E, pt=pt, mp=mp, half=half: E.matmul(
                                    psD[:, 0:512], ones2d[:, :],
                                    pt[:, half * 512:half * 512 + 512],
                                    start=(mp == 0 and half == 0),
                                    stop=(mp == 15 and half == 1), skip_group_check=True),
                                     reads=[ones2d, pt], writes=[psD], tag="smm")
                        p.op("vector", lambda E: E.reciprocal(bcst[:, :], psD[:, 0:512]),
                             reads=[psD], writes=[bcst], tag="recip")
                        p.op("vector", lambda E, h=h: E.tensor_tensor(
                            O_st[:, h * 512:(h + 1) * 512], psC[:, 0:512], bcst[:, :],
                            ALU.mult),
                             reads=[psC, bcst], writes=[O_st], tag="oscale")

                    if KDBG == q:
                        p.op("sync", lambda E: E.dma_start(dbg_q[:, :], Q_all[:, :]),
                             reads=[Q_all], writes=[("dbgq",)], dma=True)
                        p.op("sync", lambda E: E.dma_start(dbg_o[:, :], O_st[:, :]),
                             reads=[O_st], writes=[("dbgo",)], dma=True)
                        p.op("sync", lambda E: E.dma_start(dbg_k[:, :], K_fm[:, :]),
                             reads=[K_fm], writes=[("dbgk",)], dma=True)
                        p.op("sync", lambda E: E.dma_start(dbg_v[:, :], Vt[:, :]),
                             reads=[Vt], writes=[("dbgv",)], dma=True)
                        p.op("sync", lambda E: E.dma_start(dbg_c[:, :], negc[:, :]),
                             reads=[negc], writes=[("dbgc",)], dma=True)
                        p.op("sync", lambda E: E.dma_start(dbg_p[:, :], PTb[1][:, :]),
                             reads=[PTb[1]], writes=[("dbgp",)], dma=True)
                    for nsub in range(4):
                        t = q * 4 + nsub
                        xt = xb[t % 5]
                        psY = psA if nsub % 2 == 0 else psB
                        for dsub in range(2):
                            for h in range(NH):
                                p.op("tensor", lambda E, psY=psY, h=h, nsub=nsub, dsub=dsub: E.matmul(
                                    psY[:, dsub * 512:dsub * 512 + 512],
                                    O_st[:, h * 512 + nsub * 128:h * 512 + nsub * 128 + 128],
                                    poT[:, h * 1024 + dsub * 512:h * 1024 + dsub * 512 + 512],
                                    start=(h == 0), stop=(h == NH - 1), skip_group_check=True),
                                     reads=[O_st, poT], writes=[psY], tag="Ymm")
                        for dsub in range(2):
                            p.op("vector", lambda E, psY=psY, xt=xt, dsub=dsub: E.tensor_tensor(
                                xt[:, dsub * 512:dsub * 512 + 512],
                                psY[:, dsub * 512:dsub * 512 + 512],
                                xt[:, dsub * 512:dsub * 512 + 512], ALU.add),
                                 reads=[psY, xt], writes=[xt], tag="resid")
                        p.op("sync", lambda E, t=t, xt=xt: E.dma_start(
                            x_out["A"][t * 128:(t + 1) * 128, :], xt[:, :]),
                             reads=[xt], writes=[(KOUT["A"], t)], dma=True, tag="stxA")
                    if preB is not None:
                        for hc in range(q * 3, q * 3 + 3):
                            prep_wchunk(dwg, wgT_g, hc)
                            prep_wchunk(dwu, wuT_g, hc)
            p.barrier()

        def ffn_stage(st, pre=None):
            xin, xout = x_in[st], x_out[st]
            kin, kout = KIN[st], KOUT[st]
            with ExitStack() as bw:
                TB = lambda *a: bw.enter_context(nc.sbuf_tensor(*a))
                if pre is None:
                    wgT = TB(f"wgT{st}", [128, 8 * HID], BF16)
                    wuT = TB(f"wuT{st}", [128, 8 * HID], BF16)
                    wld = [TB(f"wld{st}{i}", [128, 1024], BF16) for i in range(2)]
                else:
                    wgT, wuT, wld = pre
                wdT = TB(f"wdT{st}", [128, 24 * D], BF16)
                lnq = TB(f"lnq{st}", [128, 8 * 512], BF16)
                hact = TB(f"hact{st}", [128, 24 * 512], BF16)
                gb = TB(f"gb{st}", [128, 512], BF16)
                ub = TB(f"ub{st}", [128, 512], BF16)

                for t in range(NT if _on("bstats") else 0):
                    xt = xb[t % 3]
                    p.op("sync", lambda E, t=t, xt=xt: E.dma_start(
                        xt[:, :], xin[t * 128:(t + 1) * 128, :]),
                         reads=[(kin, t)], writes=[xt], dma=True, tag="ldxS")
                    ln_stats(xt, t)

                if st == "B" and _on("bwprep"):
                    for dc in range(8):
                        for hg in range(3):
                            w = wld[(dc * 3 + hg) % 2]
                            p.op("gpsimd", lambda E, w=w, dc=dc, hg=hg: E.dma_start(
                                w[:, :], dwd[dc * 128:(dc + 1) * 128,
                                             hg * 1024:(hg + 1) * 1024]),
                                 writes=[w], dma=True, tag="ldwd")
                            for j in range(8):
                                p.op("tensor", lambda E, w=w, j=j: E.transpose(
                                    psT[:, j * 128:(j + 1) * 128],
                                    w[:, j * 128:(j + 1) * 128], ident[:, :]),
                                     reads=[w, ident], writes=[psT], tag="wdtr")
                            p.op("vector", lambda E, dc=dc, hg=hg: E.tensor_copy(
                                bass.AP(wdT, (hg * 8) * 1024 + dc * 128,
                                        [[24 * D, 128], [1024, 8], [1, 128]]),
                                psT[:, :]),
                                 reads=[psT], writes=[wdT], tag="wdev")
                elif st == "D":
                    p.op("sync", lambda E: E.dma_start(wgT[:, :], dwgt[:, :]),
                         reads=[("wgt_s",)], writes=[wgT], dma=True, tag="ldwgT")
                    p.op("sync", lambda E: E.dma_start(wuT[:, :], dwut[:, :]),
                         reads=[("wut_s",)], writes=[wuT], dma=True, tag="ldwuT")
                    p.op("sync", lambda E: E.dma_start(wdT[:, :], dwdt[:, :]),
                         reads=[("wdt_s",)], writes=[wdT], dma=True, tag="ldwdT")

                for qb in range(NQST if _on("bqloop") else 0):
                    for tt in range(4):
                        t = qb * 4 + tt
                        xt = xb[t % 5]
                        p.op("sync", lambda E, t=t, xt=xt: E.dma_start(
                            xt[:, :], xin[t * 128:(t + 1) * 128, :]),
                             reads=[(kin, t)], writes=[xt], dma=True, tag="ldxF")
                        lt = ln_tm[0]
                        ln_apply(xt, t, lt)
                        transpose8(lt)
                        p.op("vector", lambda E, tt=tt: E.tensor_copy(
                            bass.AP(lnq, tt * 128, [[8 * 512, 128], [512, 8], [1, 128]]),
                            psT[:, :]),
                             reads=[psT], writes=[lnq], tag="evac_lnq")
                    for hc in range(24):
                        ps = psA if hc % 2 == 0 else psB
                        for c in range(8):
                            p.op("tensor", lambda E, ps=ps, c=c, hc=hc: E.matmul(
                                ps[:, 0:512], wgT[:, c * HID + hc * 128:c * HID + hc * 128 + 128],
                                lnq[:, c * 512:(c + 1) * 512],
                                start=(c == 0), stop=(c == 7), skip_group_check=True),
                                 reads=[wgT, lnq], writes=[ps], tag="gmm")
                        for c in range(8):
                            p.op("tensor", lambda E, ps=ps, c=c, hc=hc: E.matmul(
                                ps[:, 512:1024], wuT[:, c * HID + hc * 128:c * HID + hc * 128 + 128],
                                lnq[:, c * 512:(c + 1) * 512],
                                start=(c == 0), stop=(c == 7), skip_group_check=True),
                                 reads=[wuT, lnq], writes=[ps], tag="umm")
                        p.op("scalar", lambda E, ps=ps: E.activation(
                            gb[:, :], ps[:, 0:512], AF.Silu, bias=0.0, scale=S127),
                             reads=[ps], writes=[gb], tag="silu")
                        p.op("scalar", lambda E, ps=ps: E.activation(
                            ub[:, :], ps[:, 512:1024], AF.Identity, bias=0.0, scale=S127),
                             reads=[ps], writes=[ub], tag="upev")
                        p.op("vector", lambda E, hc=hc: E.tensor_tensor(
                            hact[:, hc * 512:(hc + 1) * 512], gb[:, :], ub[:, :], ALU.mult),
                             reads=[gb, ub], writes=[hact], tag="hmul")
                    for nsub in range(4):
                        t = qb * 4 + nsub
                        xt = xb[t % 5]
                        psY = psA if nsub % 2 == 0 else psB
                        for dsub in range(2):
                            for hc in range(24):
                                p.op("tensor", lambda E, psY=psY, hc=hc, nsub=nsub, dsub=dsub: E.matmul(
                                    psY[:, dsub * 512:dsub * 512 + 512],
                                    hact[:, hc * 512 + nsub * 128:hc * 512 + nsub * 128 + 128],
                                    wdT[:, hc * 1024 + dsub * 512:hc * 1024 + dsub * 512 + 512],
                                    start=(hc == 0), stop=(hc == 23), skip_group_check=True),
                                     reads=[hact, wdT], writes=[psY], tag="dmm")
                        for dsub in range(2):
                            p.op("vector", lambda E, psY=psY, xt=xt, dsub=dsub: E.scalar_tensor_tensor(
                                xt[:, dsub * 512:dsub * 512 + 512],
                                psY[:, dsub * 512:dsub * 512 + 512], S127,
                                xt[:, dsub * 512:dsub * 512 + 512], ALU.mult, ALU.add),
                                 reads=[psY, xt], writes=[xt], tag="dresid")
                        p.op("sync", lambda E, t=t, xt=xt: E.dma_start(
                            xout[t * 128:(t + 1) * 128, :], xt[:, :]),
                             reads=[xt], writes=[(kout, t)], dma=True, tag="stxF")
                if st == "B":
                    p.op("sync", lambda E: E.dma_start(dwgt[:, :], wgT[:, :]),
                         reads=[wgT], writes=[("wgt_s",)], dma=True, tag="spillg")
                    p.op("sync", lambda E: E.dma_start(dwut[:, :], wuT[:, :]),
                         reads=[wuT], writes=[("wut_s",)], dma=True, tag="spillu")
                    p.op("sync", lambda E: E.dma_start(dwdt[:, :], wdT[:, :]),
                         reads=[wdT], writes=[("wdt_s",)], dma=True, tag="spilld")
            p.barrier()

        if "B" in stages:
            if "A" not in stages:
                for hc in range(24):
                    prep_wchunk(dwg, wgT_g, hc)
                    prep_wchunk(dwu, wuT_g, hc)
            ffn_stage("B", pre=preB)
            fscope.close()

        if "C" in stages:
            with ExitStack() as cw:
                TC = lambda *a: cw.enter_context(nc.sbuf_tensor(*a))
                WkT = TC("WkT", [128, 8 * 1024], BF16)
                WvT = TC("WvT", [128, 8 * 1024], BF16)
                WgoT = TC("WgoT", [128, 8 * 1024], BF16)

                QW = 4096
                keysb1 = TC("keysb", [128, QW], BF16)
                valsb2 = [TC(f"valsb{i}", [128, QW], BF16) for i in range(2)]
                knb2 = [TC(f"knb{i}", [128, QW], BF16) for i in range(2)]
                bb162 = [TC(f"bb16{i}", [128, QW], BF16) for i in range(2)]
                zb1 = TC("zb", [128, QW], BF16)
                m1_2 = [TC(f"m1_{i}", [128, 1024], BF16) for i in range(2)]
                m2_2 = [TC(f"m2_{i}", [128, 1024], BF16) for i in range(2)]
                m3_1 = TC("m3", [128, 1024], BF16)
                m1g2 = [TC(f"m1g{i}", [128, 1024], BF16) for i in range(2)]
                m2g2 = [TC(f"m2g{i}", [128, 1024], BF16) for i in range(2)]
                wldc = [m1g2[0], m1g2[1]]
                wperm = m2g2[0]
                lnC1 = TC("lnC", [128, 1024], BF16)
                bnbufC = [TC(f"bnbufC{i}", [128, 24], F32) for i in range(2)]
                lnvC = [TC(f"lnvC{i}", [128, 2], F32) for i in range(2)]
                crun = TC("crun", [1, 1024], F32)
                csnapAb = [TC(f"csnapAb{i}", [128, 1024], BF16) for i in range(2)]
                csnapBb = [TC(f"csnapBb{i}", [128, 1024], BF16) for i in range(2)]
                lfC2 = [TC(f"lfC{i}", [128, 1024], BF16) for i in range(2)]
                lfZ1 = TC("lfZ", [128, 1024], BF16)
                lnCb = TC("lnCb", [128, 1024], BF16)
                bkp = TC("bkp", [1, 1024], BF16)
                bvp = TC("bvp", [1, 1024], BF16)
                gpm = TC("gpm", [1, 1024], BF16)
                bpm = TC("bpm", [1, 1024], BF16)
                borow = TC("borow", [1, 1024], F32)
                biasCb = TC("biasCb", [1, 1024], BF16)
                gcol = TC("gcol", [128, 1], F32)
                onesC = TC("onesC", [128, 128], BF16)
                r1p = TC("r1p", [128, NT], F32)
                xbC = xb + [TC(f"xbC{i}", [128, D], F32) for i in range(3)]
                bcol = TC("bcol", [128, 1], BF16)

                def perm_row(dsrc, dst, kd):
                    p.op("sync", lambda E, dsrc=dsrc: E.dma_start(
                        crun[:, :], dsrc.rearrange("(a d) -> a d", a=1)),
                         writes=[crun], dma=True, tag="ldrow")
                    for c in range(4):
                        p.op("vector", lambda E, c=c, dst=dst: E.tensor_copy(
                            dst[:, c * 256:(c + 1) * 256],
                            bass.AP(crun, c, [[1024, 1], [4, 256]])),
                             reads=[crun], writes=[dst], tag="permrow")

                p.op("gpsimd", lambda E: E.memset(onesC[:, :], 1.0), writes=[onesC])
                perm_row(dbk, bkp, "bk")
                perm_row(dbv, bvp, "bv")
                perm_row(dlng, gpm, "lng")
                perm_row(dlnb, bpm, "lnb")
                p.op("sync", lambda E: E.dma_start(
                    borow[:, :], dbo.rearrange("(a d) -> a d", a=1)),
                     writes=[borow], dma=True, tag="ldbo")

                for nm, dsrc, dst in (("k", dWk, WkT), ("v", dWv, WvT)):
                    rsrc = dsrc.rearrange("(hh p four) i -> four hh p i", hh=2, p=128, four=4)
                    for q in range(8):
                        w = wldc[q % 2]
                        fo, hh = q // 2, q % 2
                        p.op("gpsimd", lambda E, w=w, rsrc=rsrc, fo=fo, hh=hh: E.dma_start(
                            w[:, :], rsrc[fo, hh, :, :]),
                             writes=[w], dma=True, tag="ldWkv")
                        for ci in range(8):
                            p.op("tensor", lambda E, w=w, ci=ci: E.transpose(
                                psT[:, ci * 128:(ci + 1) * 128],
                                w[:, ci * 128:(ci + 1) * 128], ident[:, :]),
                                 reads=[w, ident], writes=[psT], tag="wkvtr")
                        p.op("vector", lambda E, dst=dst, q=q: E.tensor_copy(
                            bass.AP(dst, q * 128, [[8 * 1024, 128], [1024, 8], [1, 128]]),
                            psT[:, :]),
                             reads=[psT], writes=[dst], tag="wkvev")

                for co in range(8):
                    w = wldc[co % 2]
                    p.op("gpsimd", lambda E, w=w, co=co: E.dma_start(
                        w[:, :], dWo[co * 128:(co + 1) * 128, :]),
                         writes=[w], dma=True, tag="ldWo")
                    for c in range(4):
                        p.op("vector", lambda E, w=w, c=c: E.tensor_copy(
                            wperm[:, c * 256:(c + 1) * 256],
                            bass.AP(w, c, [[1024, 128], [4, 256]])),
                             reads=[w], writes=[wperm], tag="woperm")
                    for ci in range(8):
                        p.op("tensor", lambda E, ci=ci: E.transpose(
                            psT[:, ci * 128:(ci + 1) * 128],
                            wperm[:, ci * 128:(ci + 1) * 128], ident[:, :]),
                             reads=[wperm, ident], writes=[psT], tag="wotr")
                    p.op("vector", lambda E, co=co: E.tensor_copy(
                        bass.AP(WgoT, co * 128, [[8 * 1024, 128], [1024, 8], [1, 128]]),
                        psT[:, :]),
                         reads=[psT], writes=[WgoT], tag="woev")

                for dsub in range(2):
                    for ci in range(8):
                        p.op("tensor", lambda E, ci=ci: E.matmul(
                            psD[:, 0:1], bpm[0:1, ci * 128:(ci + 1) * 128],
                            ones_row[0:1, 0:1], start=True, stop=True),
                             reads=[bpm, ones_row], writes=[psD], tag="bcolmm")
                        p.op("vector", lambda E: E.tensor_copy(bcol[:, :], psD[:, 0:1]),
                             reads=[psD], writes=[bcol], tag="bcolcp")
                        p.op("tensor", lambda E, ci=ci, dsub=dsub: E.matmul(
                            psE[0:1, 0:512], bcol[:, :],
                            WgoT[:, ci * 1024 + dsub * 512:ci * 1024 + dsub * 512 + 512],
                            start=(ci == 0), stop=(ci == 7), skip_group_check=True),
                             reads=[bcol, WgoT], writes=[psE], tag="biasmm")
                    p.op("vector", lambda E, dsub=dsub: E.tensor_tensor(
                        biasCb[0:1, dsub * 512:dsub * 512 + 512], psE[0:1, 0:512],
                        borow[0:1, dsub * 512:dsub * 512 + 512], ALU.add),
                         reads=[psE, borow], writes=[biasCb], tag="biasadd")

                for ci in range(8):
                    p.op("tensor", lambda E, ci=ci: E.matmul(
                        psD[:, 0:1], gpm[0:1, ci * 128:(ci + 1) * 128],
                        ones_row[0:1, 0:1], start=True, stop=True),
                         reads=[gpm, ones_row], writes=[psD], tag="gcolmm")
                    p.op("vector", lambda E: E.tensor_copy(gcol[:, :], psD[:, 0:1]),
                         reads=[psD], writes=[gcol], tag="gcolcp")
                    p.op("scalar", lambda E, ci=ci: E.activation(
                        WgoT[:, ci * 1024:(ci + 1) * 1024],
                        WgoT[:, ci * 1024:(ci + 1) * 1024],
                        AF.Identity, bias=0.0, scale=gcol[:, :]),
                         reads=[WgoT, gcol], writes=[WgoT], tag="gscale")


                def plane(buf, comp):
                    return bass.AP(buf, comp * 256, [[2048, 128], [1024, 2], [1, 256]])

                QUAT_BOUND = [
                    (0, [(1, 0, 0), (-1, 1, 1), (-1, 2, 2), (-1, 3, 3)]),
                    (1, [(1, 0, 1), (1, 1, 0), (1, 2, 3), (-1, 3, 2)]),
                    (2, [(1, 0, 2), (-1, 1, 3), (1, 2, 0), (1, 3, 1)]),
                    (3, [(1, 0, 3), (1, 1, 2), (-1, 2, 1), (1, 3, 0)]),
                ]
                QUAT_RETR = [
                    (0, [(1, 0, 0), (1, 1, 1), (1, 2, 2), (1, 3, 3)]),
                    (1, [(1, 1, 0), (-1, 0, 1), (-1, 2, 3), (1, 3, 2)]),
                    (2, [(1, 1, 3), (-1, 0, 2), (1, 2, 0), (-1, 3, 1)]),
                    (3, [(1, 2, 1), (-1, 0, 3), (-1, 1, 2), (1, 3, 0)]),
                ]

                def quat(table, a_buf, b_buf, out_buf, m1, m2, m1g, m2g):
                    for oc, terms in table:
                        eng = "vector" if oc < 3 else "gpsimd"
                        s1, s2 = (m1, m2) if oc < 3 else (m1g, m2g)
                        for i, (sg, ac, bc) in enumerate(terms):
                            tgt = s2 if i > 0 else s1
                            p.op(eng, lambda E, ac=ac, bc=bc, tgt=tgt,
                                 a_buf=a_buf, b_buf=b_buf: E.tensor_tensor(
                                     tgt[:, :], plane4(a_buf, ac), plane4(b_buf, bc),
                                     ALU.mult),
                                 reads=[a_buf, b_buf], writes=[tgt], tag="qmul")
                            if i > 0:
                                opx = ALU.add if sg > 0 else ALU.subtract
                                if i < 3:
                                    p.op(eng, lambda E, opx=opx, s1=s1, s2=s2: E.tensor_tensor(
                                        s1[:, :], s1[:, :], s2[:, :], opx),
                                         reads=[s1, s2], writes=[s1], tag="qacc")
                                else:
                                    p.op(eng, lambda E, opx=opx, oc=oc, s1=s1, s2=s2,
                                         out_buf=out_buf: E.tensor_tensor(
                                             plane4(out_buf, oc), s1[:, :], s2[:, :], opx),
                                         reads=[s1, s2], writes=[out_buf], tag="qout")

                p.op("vector", lambda E: E.memset(crun[:, :], 0.0), writes=[crun])
                p.op("vector", lambda E: E.memset(csnapAb[0][0:1, :], 0.0),
                     writes=[(id(csnapAb[0]), 0)])

                def cbrow(pb, u):
                    P = pb % 2
                    if u < 3:
                        return csnapAb[P], 32 * u
                    return csnapBb[P], 0

                def plane4(buf, comp):
                    return buf[:, comp * 1024:(comp + 1) * 1024]

                def tslice(buf, u, c0, nc_):
                    return bass.AP(buf, c0 * 1024 + u * 256,
                                   [[QW, 128], [1024, nc_], [1, 256]])

                def front(pb):
                    par = pb % 2
                    keysb, valsb, knb = keysb1, valsb2[par], knb2[par]
                    bb16 = bb162[par]
                    m1, m2, m3 = m1_2[par], m2_2[par], m3_1
                    m1g, m2g = m1g2[par], m2g2[par]
                    for u in range(4):
                        t = 4 * pb + u
                        xt = xbC[t % 8]
                        p.op("sync", lambda E, t=t, xt=xt: E.dma_start(
                            xt[:, :], x_in["C"][t * 128:(t + 1) * 128, :]),
                             reads=[(KIN["C"], t)], writes=[xt], dma=True, tag="ldxC")
                        ln_stats(xt, t)
                        p.op("vector", lambda E, t=t: E.tensor_scalar(
                            r1p[:, t:t + 1], rstd_all[:, t:t + 1], 1.0, None, ALU.add),
                             reads=[("rstd", t)], writes=[("r1p", t)], tag="r1p")
                        ln_apply(xt, t, lnC1 if u % 2 == 0 else lnCb)
                    for u in range(4):
                        t = 4 * pb + u
                        lt = lnC1 if u % 2 == 0 else lnCb
                        transpose8(lt)
                        lf = lfC2[u % 2]
                        p.op("scalar", lambda E, lf=lf: E.activation(
                            lf[:, :], psT[:, :], AF.Identity, bias=0.0, scale=1.0),
                             reads=[psT], writes=[lf], tag="evac_lnfmC")
                        for dst_ps, wt, brow, obuf in ((psA, WkT, bkp, keysb),
                                                       (psB, WvT, bvp, valsb)):
                            for ds in range(2):
                                for c in range(8):
                                    p.op("tensor", lambda E, dst_ps=dst_ps, wt=wt, c=c, ds=ds,
                                         lf=lf: E.matmul(
                                             dst_ps[:, ds * 512:ds * 512 + 512],
                                             lf[:, c * 128:(c + 1) * 128],
                                             wt[:, c * 1024 + ds * 512:c * 1024 + ds * 512 + 512],
                                             start=(c == 0), stop=False, skip_group_check=True),
                                         reads=[lf, wt], writes=[dst_ps], tag="kvmm")
                                p.op("tensor", lambda E, dst_ps=dst_ps, brow=brow, ds=ds: E.matmul(
                                    dst_ps[:, ds * 512:ds * 512 + 512], ones_row[:, :],
                                    brow[0:1, ds * 512:ds * 512 + 512],
                                    start=False, stop=True, skip_group_check=True),
                                     reads=[ones_row, brow], writes=[dst_ps], tag="kvbias")
                            p.op("scalar", lambda E, obuf=obuf, u=u, dst_ps=dst_ps: E.activation(
                                tslice(obuf, u, 0, 4), dst_ps[:, :],
                                AF.Identity, bias=0.0, scale=1.0),
                                 reads=[dst_ps], writes=[obuf], tag="kvev")
                    p.op("vector", lambda E, keysb=keysb, m3=m3: E.tensor_tensor(
                        m3[:, :], plane4(keysb, 0), plane4(keysb, 0), ALU.mult),
                         reads=[keysb], writes=[m3], tag="ss0")
                    for comp in range(1, 4):
                        p.op("vector", lambda E, keysb=keysb, m2=m2, comp=comp: E.tensor_tensor(
                            m2[:, :], plane4(keysb, comp), plane4(keysb, comp), ALU.mult),
                             reads=[keysb], writes=[m2], tag="ssq")
                        p.op("vector", lambda E, m2=m2, m3=m3: E.tensor_tensor(
                            m3[:, :], m3[:, :], m2[:, :], ALU.add),
                             reads=[m2, m3], writes=[m3], tag="ssacc")
                    p.op("vector", lambda E, m3=m3: E.tensor_scalar(
                        m3[:, :], m3[:, :], 1e-18, None, ALU.max),
                         reads=[m3], writes=[m3], tag="ssmax")
                    p.op("scalar", lambda E, m3=m3: E.activation(
                        m3[:, :], m3[:, :], AF.Ln, bias=0.0, scale=1.0),
                         reads=[m3], writes=[m3], tag="ssln")
                    p.op("scalar", lambda E, m3=m3: E.activation(
                        m3[:, :], m3[:, :], AF.Exp, bias=0.0, scale=-0.5),
                         reads=[m3], writes=[m3], tag="ssexp")
                    for comp in range(4):
                        p.op("vector", lambda E, comp=comp, m3=m3, knb=knb,
                             keysb=keysb: E.tensor_tensor(
                                 plane4(knb, comp), plane4(keysb, comp), m3[:, :], ALU.mult),
                             reads=[keysb, m3], writes=[knb], tag="knmul")
                    quat(QUAT_BOUND, valsb, knb, bb16, m1, m2, m1g, m2g)
                    for u in range(4):
                        for ds in range(2):
                            po = 0 if ds == 0 else 64
                            p.op("tensor", lambda E, u=u, ds=ds, po=po, bb16=bb16: E.matmul(
                                psE[po:po + 1, 0:512], ones_col[:, :],
                                tslice(bb16, u, 2 * ds, 2),
                                start=True, stop=True, skip_group_check=True),
                                 reads=[ones_col, bb16], writes=[psE], tag="Smm")
                        if u < 3:
                            db_, dbo_ = cbrow(pb, u + 1)
                        else:
                            db_, dbo_ = csnapAb[(pb + 1) % 2], 0
                        for ds in range(2):
                            po = 0 if ds == 0 else 64
                            p.op("vector", lambda E, ds=ds, po=po: E.tensor_tensor(
                                crun[0:1, ds * 512:ds * 512 + 512],
                                crun[0:1, ds * 512:ds * 512 + 512],
                                psE[po:po + 1, 0:512], ALU.add),
                                 reads=[crun, psE], writes=[crun], tag="carryupd")
                        p.op("vector", lambda E, db_=db_, dbo_=dbo_: E.tensor_copy(
                            db_[dbo_:dbo_ + 1, :], crun[:, :]),
                             reads=[crun], writes=[(id(db_), dbo_)], tag="carrybf")

                def back(pb):
                    par = pb % 2
                    valsb, knb = valsb2[par], knb2[par]
                    bb16, zb = bb162[par], zb1
                    m1, m2, m1g, m2g = m1_2[par], m2_2[par], m1g2[par], m2g2[par]
                    for u in range(4):
                        cb_, cbo = cbrow(pb, u)
                        for ds in range(2):
                            psM = psC if ds == 0 else psD
                            p.op("tensor", lambda E, psM=psM, u=u, ds=ds, bb16=bb16: E.matmul(
                                psM[:, 0:512], triu[:, :],
                                tslice(bb16, u, 2 * ds, 2),
                                start=True, stop=False, skip_group_check=True),
                                 reads=[triu, bb16], writes=[psM], tag="trilmm")
                            p.op("tensor", lambda E, psM=psM, ds=ds, cb_=cb_, cbo=cbo: E.matmul(
                                psM[:, 0:512], onesC[cbo:cbo + 1, :],
                                cb_[cbo:cbo + 1, ds * 512:ds * 512 + 512],
                                start=False, stop=True, skip_group_check=True),
                                 reads=[onesC, (id(cb_), cbo)], writes=[psM], tag="carrymm")
                        t = 4 * pb + u
                        for ds in range(2):
                            psM = psC if ds == 0 else psD
                            p.op("scalar", lambda E, psM=psM, u=u, ds=ds, t=t,
                                 valsb=valsb: E.activation(
                                     tslice(valsb, u, 2 * ds, 2),
                                     psM[:, 0:512], AF.Identity,
                                     bias=0.0, scale=rpos[:, t:t + 1]),
                                 reads=[psM, rpos], writes=[valsb], tag="Mevac")
                    quat(QUAT_RETR, valsb, knb, bb16, m1, m2, m1g, m2g)
                    retrb = bb16
                    for u in range(4):
                        t = 4 * pb + u
                        xt = xbC[t % 8]
                        sb, sv = bnbufC[par], lnvC[par]
                        for cc in range(4):
                            p.op("vector", lambda E, u=u, cc=cc, sb=sb, retrb=retrb: E.bn_stats(
                                sb[:, cc * 6:cc * 6 + 6],
                                retrb[:, cc * 1024 + u * 256:cc * 1024 + u * 256 + 256]),
                                 reads=[retrb], writes=[sb], tag="bnsr")
                        p.op("vector", lambda E, sb=sb, sv=sv: E.bn_aggr(sv[:, :], sb[:, :]),
                             reads=[sb], writes=[sv], tag="bnar")
                        j2 = NT + t
                        p.op("scalar", lambda E, j2=j2, sv=sv: E.activation(
                            rstd_all[:, j2:j2 + 1], sv[:, 1:2], AF.Ln, bias=ceps, scale=1.0),
                             reads=[sv, ckey(EPS)], writes=[("rstd", j2)], tag="rstdC1")
                        p.op("scalar", lambda E, j2=j2: E.activation(
                            rstd_all[:, j2:j2 + 1], rstd_all[:, j2:j2 + 1], AF.Exp,
                            bias=0.0, scale=-0.5),
                             reads=[("rstd", j2)], writes=[("rstd", j2)], tag="rstdC2")
                        p.op("vector", lambda E, j2=j2, sv=sv: E.scalar_tensor_tensor(
                            nmr_all[:, j2:j2 + 1], sv[:, 0:1], -1.0, rstd_all[:, j2:j2 + 1],
                            ALU.mult, ALU.mult),
                             reads=[sv, ("rstd", j2)], writes=[("nmr", j2)], tag="nmrC")
                        p.op("scalar", lambda E, u=u, j2=j2, zb=zb,
                             retrb=retrb: E.activation(
                                 zb[:, u * 1024:(u + 1) * 1024], tslice(retrb, u, 0, 4),
                                 AF.Identity,
                                 bias=nmr_all[:, j2:j2 + 1], scale=rstd_all[:, j2:j2 + 1]),
                             reads=[retrb, ("rstd", j2), ("nmr", j2)], writes=[zb], tag="zapply")
                        for c in range(8):
                            p.op("tensor", lambda E, c=c, u=u, zb=zb: E.transpose(
                                psT[:, c * 128:(c + 1) * 128],
                                zb[:, u * 1024 + c * 128:u * 1024 + c * 128 + 128],
                                ident[:, :]),
                                 reads=[zb, ident], writes=[psT], tag="ztr")
                        lf = lfZ1
                        p.op("scalar", lambda E, lf=lf: E.activation(
                            lf[:, :], psT[:, :], AF.Identity, bias=0.0, scale=1.0),
                             reads=[psT], writes=[lf], tag="zfmev")
                        for ds in range(2):
                            psO = psC if ds == 0 else psD
                            for c in range(8):
                                p.op("tensor", lambda E, psO=psO, c=c, ds=ds, lf=lf: E.matmul(
                                    psO[:, 0:512],
                                    lf[:, c * 128:(c + 1) * 128],
                                    WgoT[:, c * 1024 + ds * 512:c * 1024 + ds * 512 + 512],
                                    start=(c == 0), stop=False, skip_group_check=True),
                                     reads=[lf, WgoT], writes=[psO], tag="gomm")
                            p.op("tensor", lambda E, psO=psO, ds=ds: E.matmul(
                                psO[:, 0:512], ones_row[:, :],
                                biasCb[0:1, ds * 512:ds * 512 + 512],
                                start=False, stop=True, skip_group_check=True),
                                 reads=[ones_row, biasCb], writes=[psO], tag="gobias")
                        p.op("vector", lambda E, xt=xt, t=t: E.tensor_scalar(
                            xt[:, :], xt[:, :], r1p[:, t:t + 1], nmr_all[:, t:t + 1],
                            ALU.mult, ALU.add),
                             reads=[xt, ("r1p", t), ("nmr", t)], writes=[xt], tag="residC0")
                        for ds in range(2):
                            psO = psC if ds == 0 else psD
                            p.op("vector", lambda E, psO=psO, xt=xt, ds=ds: E.tensor_tensor(
                                xt[:, ds * 512:ds * 512 + 512],
                                psO[:, 0:512],
                                xt[:, ds * 512:ds * 512 + 512], ALU.add),
                                 reads=[psO, xt], writes=[xt], tag="residC1")
                        p.op("sync", lambda E, t=t, xt=xt: E.dma_start(
                            x_out["C"][t * 128:(t + 1) * 128, :], xt[:, :]),
                             reads=[xt], writes=[(KOUT["C"], t)], dma=True, tag="stxC")

                front(0)
                for pb in range(8):
                    if pb + 1 < 8:
                        front(pb + 1)
                    back(pb)
            p.barrier()

        if "D" in stages:
            ffn_stage("D")

        p.barrier()
        p.emit(block, sems)
    return nc


def _shard_inputs(inputs):
    per_core = []
    x = np.ascontiguousarray(inputs["L0_attn_a_x"])
    shared = {
        "pq": np.ascontiguousarray(inputs["L0_attn_a_proj_q"]),
        "pk": np.ascontiguousarray(inputs["L0_attn_a_proj_k"]),
        "pv": np.ascontiguousarray(inputs["L0_attn_a_proj_v"]),
        "po": np.ascontiguousarray(inputs["L0_attn_a_proj_o"]),
        "Wk": np.ascontiguousarray(inputs["L0_attn_b_Wk"]),
        "bk": np.ascontiguousarray(inputs["L0_attn_b_bk"]),
        "Wv": np.ascontiguousarray(inputs["L0_attn_b_Wv"]),
        "bv": np.ascontiguousarray(inputs["L0_attn_b_bv"]),
        "lng": np.ascontiguousarray(inputs["L0_attn_b_ln_g"]),
        "lnb": np.ascontiguousarray(inputs["L0_attn_b_ln_b"]),
        "Wo": np.ascontiguousarray(inputs["L0_attn_b_Wo"]),
        "bo": np.ascontiguousarray(inputs["L0_attn_b_bo"]),
        "wg": np.ascontiguousarray(inputs["L0_ffn_w_gate"]),
        "wu": np.ascontiguousarray(inputs["L0_ffn_w_up"]),
        "wd": np.ascontiguousarray(inputs["L0_ffn_w_down"]),
    }
    for b in range(8):
        m = {"x0": np.ascontiguousarray(x[b])}
        m.update(shared)
        per_core.append(m)
    return per_core


def kernel(**inputs):
    nc = build()
    in_maps = _shard_inputs(inputs)
    res = run_bass_kernel_spmd(nc, in_maps, core_ids=list(range(8)))
    return np.stack([r["y"] for r in res.results], axis=0)


# version 15
# speedup vs baseline: 1.0276x; 1.0105x over previous
import os
import numpy as np
from contextlib import ExitStack
_KPARTS = set(os.environ.get("KPARTS", "all").split(","))
def _on(x):
    return "all" in _KPARTS or x in _KPARTS
import concourse.bass as bass
import concourse.mybir as mybir
from concourse.bass_utils import run_bass_kernel_spmd

F32 = mybir.dt.float32
BF16 = mybir.dt.bfloat16
I32 = mybir.dt.int32
AF = mybir.ActivationFunctionType
ALU = mybir.AluOpType
AX = mybir.AxisListType

ENGINES = ("sync", "tensor", "vector", "scalar", "gpsimd")

S, D, HID, NH, DK = 4096, 1024, 3072, 8, 128
NT = S // 128
NQST = 8
EPS = 1e-5
S127 = float(np.float32(1.0 / 127.0))


class _Op:
    __slots__ = ("eng", "fn", "dma", "idx", "needs_inc", "inc_value", "deps", "tag", "lane")

    def __init__(self, eng, fn, dma, idx, tag=""):
        self.eng, self.fn, self.dma, self.idx, self.tag = eng, fn, dma, idx, tag
        self.needs_inc = False
        self.inc_value = None
        self.deps = []
        self.lane = None


NLANES = 6


class Prog:
    def __init__(self, nc):
        self.nc = nc
        self.ops = {e: [] for e in ENGINES}
        self.last_w = {}
        self.readers = {}
        self.lane_rr = {}

    @staticmethod
    def _k(b):
        return b if isinstance(b, (str, tuple)) else id(b)

    def op(self, eng, fn, reads=(), writes=(), dma=False, tag=""):
        if dma:
            lane = self.lane_rr.get(eng, 0)
            self.lane_rr[eng] = (lane + 1) % NLANES
            writes = list(writes) + [("_lane", eng, lane)]
        o = _Op(eng, fn, dma, len(self.ops[eng]), tag)
        if dma:
            o.lane = (eng, lane)
        deps = {}

        def add(d):
            if d is None:
                return
            if d.eng == eng and not d.dma and eng == "tensor":
                return
            deps[(d.eng, d.idx)] = d

        for b in reads:
            add(self.last_w.get(self._k(b)))
        for b in writes:
            k = self._k(b)
            add(self.last_w.get(k))
            latest = {}
            for r in self.readers.get(k, ()):
                if r.dma:
                    add(r)
                elif r.idx >= latest.get(r.eng, (-1, None))[0]:
                    latest[r.eng] = (r.idx, r)
            for _, r in latest.values():
                add(r)
        o.deps = list(deps.values())
        self.ops[eng].append(o)
        for b in reads:
            self.readers.setdefault(self._k(b), []).append(o)
        for b in writes:
            k = self._k(b)
            self.last_w[k] = o
            self.readers[k] = []
        return o

    def barrier(self):
        lasts = []
        for e in ENGINES:
            if self.ops[e]:
                lasts.append(self.ops[e][-1])
                seen = set()
                for o in reversed(self.ops[e]):
                    if o.dma and o.lane not in seen:
                        seen.add(o.lane)
                        lasts.append(o)
        for e in ENGINES:
            o = _Op(e, (lambda E: E.nop()), False, len(self.ops[e]), "barrier")
            o.deps = [d for d in lasts if not (d.eng == e and not d.dma)]
            self.ops[e].append(o)

    def emit(self, block, sems):
        for e in ENGINES:
            for o in self.ops[e]:
                if o.dma:
                    o.needs_inc = True
                for d in o.deps:
                    d.needs_inc = True
        lane_c = {}
        for e in ENGINES:
            c = 0
            for o in self.ops[e]:
                if o.dma:
                    lc = lane_c.get(o.lane, 0) + 16
                    lane_c[o.lane] = lc
                    o.inc_value = lc
                elif o.needs_inc:
                    c += 1
                    o.inc_value = c

        def section(eng):
            def body(E):
                waited = {}
                for o in self.ops[eng]:
                    need = {}
                    for d in o.deps:
                        key = d.lane if d.dma else d.eng
                        if d.inc_value > need.get(key, 0):
                            need[key] = d.inc_value
                    for key, v in need.items():
                        if waited.get(key, 0) < v:
                            waited[key] = v
                            E.wait_ge(sems[key], v)
                    inst = o.fn(E)
                    if o.dma:
                        inst.then_inc(sems[o.lane], 16)
                    elif o.needs_inc:
                        inst.then_inc(sems[o.eng], 1)
            return body

        for e in ENGINES:
            if self.ops[e]:
                getattr(block, e)(section(e))


def build(stages=("A", "B", "C", "D")):
    nc = bass.Bass("TRN2", target_bir_lowering=False, dynamic_dma_scratch_size=16384)

    dx0 = nc.dram_tensor("x0", [S, D], F32, kind="ExternalInput")
    dpq = nc.dram_tensor("pq", [NH, D, DK], F32, kind="ExternalInput")
    dpk = nc.dram_tensor("pk", [D, DK], F32, kind="ExternalInput")
    dpv = nc.dram_tensor("pv", [D, DK], F32, kind="ExternalInput")
    dpo = nc.dram_tensor("po", [NH, D, DK], F32, kind="ExternalInput")
    dWk = nc.dram_tensor("Wk", [D, D], F32, kind="ExternalInput")
    dbk = nc.dram_tensor("bk", [D], F32, kind="ExternalInput")
    dWv = nc.dram_tensor("Wv", [D, D], F32, kind="ExternalInput")
    dbv = nc.dram_tensor("bv", [D], F32, kind="ExternalInput")
    dlng = nc.dram_tensor("lng", [D], F32, kind="ExternalInput")
    dlnb = nc.dram_tensor("lnb", [D], F32, kind="ExternalInput")
    dWo = nc.dram_tensor("Wo", [D, D], F32, kind="ExternalInput")
    dbo = nc.dram_tensor("bo", [D], F32, kind="ExternalInput")
    dwg = nc.dram_tensor("wg", [HID, D], I32, kind="ExternalInput")
    dwu = nc.dram_tensor("wu", [HID, D], I32, kind="ExternalInput")
    dwd = nc.dram_tensor("wd", [D, HID], I32, kind="ExternalInput")
    dy = nc.dram_tensor("y", [S, D], F32, kind="ExternalOutput")

    KDBG = int(os.environ.get("KDBG", "-1"))
    if KDBG >= 0:
        dbg_q = nc.dram_tensor("dbg_q", [128, NH * 512], BF16, kind="ExternalOutput")
        dbg_o = nc.dram_tensor("dbg_o", [128, NH * 512], BF16, kind="ExternalOutput")
        dbg_k = nc.dram_tensor("dbg_k", [128, S], BF16, kind="ExternalOutput")
        dbg_v = nc.dram_tensor("dbg_v", [128, NT * 130], BF16, kind="ExternalOutput")
        dbg_c = nc.dram_tensor("dbg_c", [128, 1], F32, kind="ExternalOutput")
        dbg_p = nc.dram_tensor("dbg_p", [128, 1024], BF16, kind="ExternalOutput")
        dbg_lt = nc.dram_tensor("dbg_lt", [128, D], BF16, kind="ExternalOutput")
        dbg_xt = nc.dram_tensor("dbg_xt", [128, D], F32, kind="ExternalOutput")
        dbg_st = nc.dram_tensor("dbg_st", [128, 3], F32, kind="ExternalOutput")
        dbg_lq = nc.dram_tensor("dbg_lq", [128, 8 * 512], BF16, kind="ExternalOutput")
    dx1 = nc.dram_tensor("x1s", [S, D], F32)
    dx2 = nc.dram_tensor("x2s", [S, D], F32)
    dx3 = nc.dram_tensor("x3s", [S, D], F32)
    dwgt = nc.dram_tensor("wgt_s", [128, 8 * HID], BF16)
    dwut = nc.dram_tensor("wut_s", [128, 8 * HID], BF16)
    dwdt = nc.dram_tensor("wdt_s", [128, 24 * D], BF16)

    p = Prog(nc)
    x_in = {"A": dx0, "B": dx1, "C": dx2, "D": dx3}
    x_out = {"A": dx1, "B": dx2, "C": dx3, "D": dy}
    x_out[stages[-1]] = dy
    KIN = {"A": "x0", "B": "x1", "C": "x2", "D": "x3"}
    KOUT = {"A": "x1", "B": "x2", "C": "x3", "D": "y"}
    KOUT[stages[-1]] = "y"

    with ExitStack() as g:
        T = lambda *a, **k: g.enter_context(nc.sbuf_tensor(*a, **k))
        consts = T("consts", [128, 8], F32)
        ones_col = T("ones_col", [128, 1], BF16)
        ones_row = T("ones_row", [1, 128], BF16)
        ident = T("ident", [128, 128], BF16)
        triu = T("triu", [128, 128], BF16)
        xb = [T(f"xb{i}", [128, D], F32) for i in range(5)]
        mu_all = T("mu_all", [128, NT], F32)
        rstd_all = T("rstd_all", [128, 2 * NT], F32)
        nmr_all = T("nmr_all", [128, 2 * NT], F32)
        bnbuf = T("bnbuf", [128, 12], F32)
        lnv = T("lnv", [128, 2], F32)
        ln_tm = [T(f"ln_tm{i}", [128, D], BF16) for i in range(1)]
        rpos = T("rpos", [128, NT], F32)
        smalls = T("smalls", [1, 8], F32)
        smallsb = T("smallsb", [1, 4], BF16)

        psA = g.enter_context(nc.psum_tensor("psA", [128, 1024], F32))
        psB = g.enter_context(nc.psum_tensor("psB", [128, 1024], F32))
        psC = g.enter_context(nc.psum_tensor("psC", [128, 512], F32))
        psD = g.enter_context(nc.psum_tensor("psD", [128, 512], F32))
        psE = g.enter_context(nc.psum_tensor("psE", [128, 512], F32))
        psT = g.enter_context(nc.psum_tensor("psT", [128, 1024], BF16))

        sems = {e: g.enter_context(nc.semaphore("m_" + e)) for e in ENGINES}
        for e in ("sync", "gpsimd"):
            for ln in range(6):
                sems[(e, ln)] = g.enter_context(nc.semaphore(f"d_{e}{ln}"))
        block = g.enter_context(nc.Block())

        cvals = {}

        def ccol(v):
            v = float(v)
            if v not in cvals:
                j = len(cvals)
                ap = consts[:, j:j + 1]
                p.op("gpsimd", lambda E, ap=ap, v=v: E.memset(ap, v),
                     writes=[("const", j)], tag="const")
                cvals[v] = (ap, ("const", j))
            return cvals[v][0]

        def ckey(v):
            return cvals[float(v)][1]

        ceps = ccol(EPS)

        p.op("gpsimd", lambda E: E.memset(ones_col[:, :], 1.0), writes=[ones_col])
        p.op("gpsimd", lambda E: E.memset(ones_row[:, :], 1.0), writes=[ones_row])
        p.op("gpsimd", lambda E: E.memset(ident[:, :], 1.0), writes=[ident])
        p.op("gpsimd", lambda E: E.affine_select(out=ident[:, :], in_=ident[:, :],
                                                 compare_op=ALU.is_equal, fill=0.0, base=0,
                                                 pattern=[[-1, 128]], channel_multiplier=1),
             reads=[ident], writes=[ident])
        p.op("gpsimd", lambda E: E.memset(triu[:, :], 0.0), writes=[triu])
        p.op("gpsimd", lambda E: E.affine_select(out=triu[:, :], in_=triu[:, :],
                                                 compare_op=ALU.is_gt, fill=1.0, base=0,
                                                 pattern=[[-1, 128]], channel_multiplier=1),
             reads=[triu], writes=[triu])
        posi = T("posi", [128, NT], I32)
        p.op("gpsimd", lambda E: E.iota(posi[:, :], pattern=[[128, NT]], base=1,
                                        channel_multiplier=1), writes=[posi])
        p.op("vector", lambda E: E.tensor_copy(rpos[:, :], posi[:, :]),
             reads=[posi], writes=[rpos])
        p.op("scalar", lambda E: E.activation(rpos[:, :], rpos[:, :], AF.Ln, bias=0.0, scale=1.0),
             reads=[rpos], writes=[rpos])
        p.op("scalar", lambda E: E.activation(rpos[:, :], rpos[:, :], AF.Exp, bias=0.0, scale=-0.5),
             reads=[rpos], writes=[rpos])

        def ln_stats(xtile, j, sb=None, sv=None):
            sb = bnbuf if sb is None else sb
            sv = lnv if sv is None else sv
            p.op("vector", lambda E, sb=sb: E.bn_stats(sb[:, 0:6], xtile[:, 0:512]),
                 reads=[xtile], writes=[sb], tag="bns0")
            p.op("vector", lambda E, sb=sb: E.bn_stats(sb[:, 6:12], xtile[:, 512:1024]),
                 reads=[xtile], writes=[sb], tag="bns1")
            p.op("vector", lambda E, sb=sb, sv=sv: E.bn_aggr(sv[:, :], sb[:, :]),
                 reads=[sb], writes=[sv], tag="bna")
            p.op("vector", lambda E, sv=sv: E.tensor_copy(mu_all[:, j:j + 1], sv[:, 0:1]),
                 reads=[sv], writes=[("mu", j)])
            p.op("scalar", lambda E, sv=sv: E.activation(rstd_all[:, j:j + 1], sv[:, 1:2],
                                                         AF.Ln, bias=ceps, scale=1.0),
                 reads=[sv, ckey(EPS)], writes=[("rstd", j)], tag="ln_rstd")
            p.op("scalar", lambda E: E.activation(rstd_all[:, j:j + 1], rstd_all[:, j:j + 1],
                                                  AF.Exp, bias=0.0, scale=-0.5),
                 reads=[("rstd", j)], writes=[("rstd", j)])
            p.op("vector", lambda E: E.scalar_tensor_tensor(
                nmr_all[:, j:j + 1], mu_all[:, j:j + 1], -1.0, rstd_all[:, j:j + 1],
                ALU.mult, ALU.mult),
                 reads=[("mu", j), ("rstd", j)], writes=[("nmr", j)], tag="nmr")

        def ln_apply(xtile, j, out):
            p.op("scalar", lambda E: E.activation(out[:, :], xtile[:, :], AF.Identity,
                                                  bias=nmr_all[:, j:j + 1],
                                                  scale=rstd_all[:, j:j + 1]),
                 reads=[xtile, ("rstd", j), ("nmr", j)], writes=[out], tag="ln_apply")

        def transpose8(src):
            for c in range(8):
                p.op("tensor", lambda E, c=c: E.transpose(psT[:, c * 128:(c + 1) * 128],
                                                          src[:, c * 128:(c + 1) * 128],
                                                          ident[:, :]),
                     reads=[src, ident], writes=[psT], tag="tr")

        fscope = ExitStack()
        preB = None
        if "B" in stages:
            wgT_g = fscope.enter_context(nc.sbuf_tensor("wgT_g", [128, 8 * HID], BF16))
            wuT_g = fscope.enter_context(nc.sbuf_tensor("wuT_g", [128, 8 * HID], BF16))
            wld_g = [fscope.enter_context(nc.sbuf_tensor(f"wld_g{i}", [128, 1024], BF16))
                     for i in range(2)]
            preB = (wgT_g, wuT_g, wld_g)

            def prep_wchunk(dsrc, dst, hc):
                w = wld_g[hc % 2]
                p.op("gpsimd", lambda E, w=w, dsrc=dsrc, hc=hc: E.dma_start(
                    w[:, 0:1024], dsrc[hc * 128:(hc + 1) * 128, :]),
                     writes=[w], dma=True, tag="ldw")
                for c in range(8):
                    p.op("tensor", lambda E, w=w, c=c: E.transpose(
                        psT[:, c * 128:(c + 1) * 128],
                        w[:, c * 128:(c + 1) * 128], ident[:, :]),
                         reads=[w, ident], writes=[psT], tag="wtr")
                p.op("vector", lambda E, dst=dst, hc=hc: E.tensor_copy(
                    bass.AP(dst, hc * 128, [[8 * HID, 128], [HID, 8], [1, 128]]),
                    psT[:, :]),
                     reads=[psT], writes=[dst], tag="wev")

        if "A" in stages:
            with ExitStack() as aw:
                TA = lambda *a: aw.enter_context(nc.sbuf_tensor(*a))
                K_fm = TA("K_fm", [128, S], BF16)
                Vt = TA("Vt", [128, NT * 130], BF16)
                pq_sb = TA("pq_sb", [128, NH * 8 * 128], BF16)
                pk_sb = TA("pk_sb", [128, 8 * 128], BF16)
                pv_sb = TA("pv_sb", [128, 8 * 128], BF16)
                poT = TA("poT", [128, NH * 1024], BF16)
                lnfm_q = TA("lnfm_q", [128, 8 * 512], BF16)
                Q_all = TA("Q_all", [128, NH * 512], BF16)
                PTb = [TA(f"PTb{i}", [128, 1024], BF16) for i in range(2)]
                Qsq = TA("Qsq", [128, 512], BF16)
                O_st = TA("O_st", [128, NH * 512], BF16)
                bcst = TA("bcst", [128, 512], F32)
                negc = TA("negc", [128, 1], F32)
                ksq_s = TA("ksq_s", [128, 128], BF16)
                mxk = TA("mxk", [1, 1], F32)
                ones2d = TA("ones2d", [128, 128], BF16)

                p.op("gpsimd", lambda E: E.dma_start(
                    pk_sb[:, :].rearrange("p (c k) -> p c k", c=8),
                    dpk.rearrange("(c p) k -> p c k", p=128)),
                     writes=[pk_sb], dma=True, tag="ld_pk")
                p.op("gpsimd", lambda E: E.dma_start(
                    pv_sb[:, :].rearrange("p (c k) -> p c k", c=8),
                    dpv.rearrange("(c p) k -> p c k", p=128)),
                     writes=[pv_sb], dma=True, tag="ld_pv")
                for hh in range(2):
                    p.op("gpsimd", lambda E, hh=hh: E.dma_start(
                        pq_sb[:, hh * 4096:(hh + 1) * 4096].rearrange(
                            "p (h c k) -> p h c k", h=4, c=8),
                        dpq[hh * 4:(hh + 1) * 4].rearrange("h (c p) k -> p h c k", p=128)),
                         writes=[pq_sb], dma=True, tag="ld_pq")

                p.op("gpsimd", lambda E: E.memset(
                    bass.AP(Vt, 128, [[NT * 130, 128], [130, NT], [1, 1]]), 1.0),
                     writes=[Vt], tag="vt_ones")

                p.op("vector", lambda E: E.memset(mxk[:, :], 0.0), writes=[mxk])
                p.op("gpsimd", lambda E: E.memset(ones2d[:, :], 1.0), writes=[ones2d])

                for t in range(NT):
                    xt = xb[t % 3]
                    p.op("sync", lambda E, t=t, xt=xt: E.dma_start(
                        xt[:, :], x_in["A"][t * 128:(t + 1) * 128, :]),
                         reads=[(KIN["A"], t)], writes=[xt], dma=True, tag="ldxA1")
                    ln_stats(xt, t)
                    lt = ln_tm[0]
                    ln_apply(xt, t, lt)
                    transpose8(lt)
                    lf = PTb[0]
                    p.op("vector", lambda E, lf=lf: E.tensor_copy(lf[:, :], psT[:, :]),
                         reads=[psT], writes=[lf], tag="evac_lnfm")
                    for c in range(8):
                        p.op("tensor", lambda E, c=c, lf=lf: E.matmul(
                            psC[:, 0:128], pk_sb[:, c * 128:(c + 1) * 128],
                            lf[:, c * 128:(c + 1) * 128],
                            start=(c == 0), stop=(c == 7)),
                             reads=[pk_sb, lf], writes=[psC], tag="Kmm")
                    p.op("vector", lambda E, t=t: E.tensor_copy(
                        K_fm[:, t * 128:(t + 1) * 128], psC[:, 0:128]),
                         reads=[psC], writes=[K_fm], tag="evacK")
                    for c in range(8):
                        p.op("tensor", lambda E, c=c, lf=lf: E.matmul(
                            psD[:, 0:128], lf[:, c * 128:(c + 1) * 128],
                            pv_sb[:, c * 128:(c + 1) * 128],
                            start=(c == 0), stop=(c == 7)),
                             reads=[pv_sb, lf], writes=[psD], tag="Vmm")
                    p.op("vector", lambda E, t=t: E.tensor_copy(
                        Vt[:, t * 130:t * 130 + 128], psD[:, 0:128]),
                         reads=[psD], writes=[Vt], tag="evacV")
                    p.op("vector", lambda E, t=t: E.tensor_tensor(
                        ksq_s[:, :], K_fm[:, t * 128:(t + 1) * 128],
                        K_fm[:, t * 128:(t + 1) * 128], ALU.mult),
                         reads=[K_fm], writes=[ksq_s], tag="ksq")
                    p.op("tensor", lambda E: E.matmul(psE[0:1, 0:128], ones_col[:, :],
                                                      ksq_s[:, :], start=True, stop=True),
                         reads=[ones_col, ksq_s], writes=[psE], tag="ksqmm")
                    p.op("vector", lambda E: E.tensor_reduce(smalls[:, 0:1], psE[0:1, 0:128],
                                                             AX.X, ALU.max),
                         reads=[psE], writes=[smalls], tag="ksqmax")
                    p.op("vector", lambda E: E.tensor_tensor(mxk[:, :], mxk[:, :],
                                                             smalls[:, 0:1], ALU.max),
                         reads=[mxk, smalls], writes=[mxk], tag="mxk")

                for hh in range(2):
                    p.op("gpsimd", lambda E, hh=hh: E.dma_start(
                        Q_all[:, :].rearrange("p (h c k) -> p h c k", h=4, c=8),
                        dpo[hh * 4:(hh + 1) * 4].rearrange("h (c p) k -> p h c k", p=128)),
                         writes=[Q_all], dma=True, tag="ld_po")
                    for h4 in range(4):
                        h = hh * 4 + h4
                        for c in range(8):
                            p.op("tensor", lambda E, h4=h4, c=c: E.transpose(
                                psT[:, c * 128:c * 128 + 128],
                                Q_all[:, (h4 * 8 + c) * 128:(h4 * 8 + c) * 128 + 128],
                                ident[:, :]),
                                 reads=[Q_all, ident], writes=[psT], tag="potr")
                        p.op("vector", lambda E, h=h: E.tensor_copy(
                            poT[:, h * 1024:(h + 1) * 1024], psT[:, :]),
                             reads=[psT], writes=[poT], tag="evac_poT")

                lnq_view = lnfm_q[:, :].rearrange("p (c n) -> p c n", c=8)
                for q in range(NQST):
                    for tt in range(4):
                        t = q * 4 + tt
                        xt = xb[t % 5]
                        p.op("sync", lambda E, t=t, xt=xt: E.dma_start(
                            xt[:, :], x_in["A"][t * 128:(t + 1) * 128, :]),
                             reads=[(KIN["A"], t)], writes=[xt], dma=True, tag="ldxA2")
                        lt = ln_tm[0]
                        ln_apply(xt, t, lt)
                        transpose8(lt)
                        p.op("vector", lambda E, tt=tt: E.tensor_copy(
                            bass.AP(lnfm_q, tt * 128, [[8 * 512, 128], [512, 8], [1, 128]]),
                            psT[:, :]),
                             reads=[psT], writes=[lnfm_q], tag="evac_lnfmq")
                        if KDBG == q and tt == 0:
                            p.op("sync", lambda E, lt=lt: E.dma_start(dbg_lt[:, :], lt[:, :]),
                                 reads=[lt], writes=[("dbglt",)], dma=True)
                            p.op("sync", lambda E, xt=xt: E.dma_start(dbg_xt[:, :], xt[:, :]),
                                 reads=[xt], writes=[("dbgxt",)], dma=True)
                            p.op("sync", lambda E, t=t: E.dma_start(dbg_st[:, 0:1], mu_all[:, t:t + 1], allow_slow_non_contiguous=True),
                                 reads=[mu_all], writes=[("dbgst",)], dma=True)
                            p.op("sync", lambda E, t=t: E.dma_start(dbg_st[:, 1:2], rstd_all[:, t:t + 1], allow_slow_non_contiguous=True),
                                 reads=[rstd_all], writes=[("dbgst",)], dma=True)
                            p.op("sync", lambda E, t=t: E.dma_start(dbg_st[:, 2:3], nmr_all[:, t:t + 1], allow_slow_non_contiguous=True),
                                 reads=[nmr_all], writes=[("dbgst",)], dma=True)
                    if KDBG == q:
                        p.op("sync", lambda E: E.dma_start(dbg_lq[:, :], lnfm_q[:, :]),
                             reads=[lnfm_q], writes=[("dbglq",)], dma=True)
                    p.op("vector", lambda E: E.memset(smalls[:, 1:2], 0.0), writes=[smalls])
                    for h in range(NH):
                        ps = psD if h % 2 == 0 else psE
                        for c in range(8):
                            p.op("tensor", lambda E, h=h, c=c, ps=ps: E.matmul(
                                ps[:, 0:512], pq_sb[:, (h * 8 + c) * 128:(h * 8 + c) * 128 + 128],
                                lnfm_q[:, c * 512:(c + 1) * 512],
                                start=(c == 0), stop=(c == 7)),
                                 reads=[pq_sb, lnfm_q], writes=[ps], tag="Qmm")
                        p.op("vector", lambda E, h=h, ps=ps: E.tensor_copy(
                            Q_all[:, h * 512:(h + 1) * 512], ps[:, 0:512]),
                             reads=[ps], writes=[Q_all], tag="evacQ")
                        p.op("vector", lambda E, h=h: E.tensor_tensor(
                            Qsq[:, :], Q_all[:, h * 512:(h + 1) * 512],
                            Q_all[:, h * 512:(h + 1) * 512], ALU.mult),
                             reads=[Q_all], writes=[Qsq], tag="qsq")
                        p.op("tensor", lambda E: E.matmul(psC[0:1, 0:512], ones_col[:, :],
                                                          Qsq[:, :], start=True, stop=True),
                             reads=[ones_col, Qsq], writes=[psC], tag="qsqmm")
                        p.op("vector", lambda E: E.tensor_reduce(
                            smalls[:, 2:3], psC[0:1, 0:512], AX.X, ALU.max),
                             reads=[psC], writes=[smalls], tag="qsqmax")
                        p.op("vector", lambda E: E.tensor_tensor(
                            smalls[:, 1:2], smalls[:, 1:2], smalls[:, 2:3], ALU.max),
                             reads=[smalls], writes=[smalls], tag="mxq")
                    p.op("vector", lambda E: E.tensor_tensor(
                        smalls[:, 3:4], smalls[:, 1:2], mxk[:, :], ALU.mult),
                         reads=[smalls, mxk], writes=[smalls], tag="csq")
                    p.op("scalar", lambda E: E.activation(smalls[:, 4:5], smalls[:, 3:4],
                                                          AF.Ln, bias=0.0, scale=1.0),
                         reads=[smalls], writes=[smalls])
                    p.op("scalar", lambda E: E.activation(smalls[:, 4:5], smalls[:, 4:5],
                                                          AF.Exp, bias=0.0, scale=0.5),
                         reads=[smalls], writes=[smalls])
                    p.op("vector", lambda E: E.tensor_scalar(
                        smallsb[:, 0:1], smalls[:, 4:5], -1.0, None, ALU.mult),
                         reads=[smalls], writes=[smallsb], tag="negc_bf")
                    p.op("tensor", lambda E: E.matmul(psC[:, 0:1], ones_row[:, :],
                                                      smallsb[:, 0:1], start=True, stop=True),
                         reads=[ones_row, smallsb], writes=[psC], tag="negc_bc")
                    p.op("vector", lambda E: E.tensor_copy(negc[:, :], psC[:, 0:1]),
                         reads=[psC], writes=[negc], tag="negc_cp")

                    def emit_qk(h, mp):
                        ps = psA if mp % 2 == 0 else psB
                        for half in range(2):
                            mt = 2 * mp + half
                            p.op("tensor", lambda E, ps=ps, mt=mt, half=half, h=h: E.matmul(
                                ps[:, half * 512:half * 512 + 512],
                                K_fm[:, mt * 128:(mt + 1) * 128],
                                Q_all[:, h * 512:(h + 1) * 512],
                                start=True, stop=True, skip_group_check=True),
                                 reads=[K_fm, Q_all], writes=[ps], tag="QK")

                    for h in range(NH):
                        if h == 0:
                            emit_qk(h, 0)
                        for mp in range(16):
                            ps = psA if mp % 2 == 0 else psB
                            pt = PTb[mp % 2]
                            if mp + 1 < 16:
                                emit_qk(h, mp + 1)
                            elif h + 1 < NH:
                                emit_qk(h + 1, 0)
                            p.op("scalar", lambda E, pt=pt, ps=ps: E.activation(
                                pt[:, :], ps[:, :], AF.Exp, bias=negc[:, :], scale=1.0),
                                 reads=[ps, negc], writes=[pt], tag="exp")
                            for half in range(2):
                                mt = 2 * mp + half
                                p.op("tensor", lambda E, pt=pt, mt=mt, mp=mp, half=half: E.matmul(
                                    psC[:, 0:512], Vt[:, mt * 130:mt * 130 + 128],
                                    pt[:, half * 512:half * 512 + 512],
                                    start=(mp == 0 and half == 0),
                                    stop=(mp == 15 and half == 1), skip_group_check=True),
                                     reads=[Vt, pt], writes=[psC], tag="PV")
                                p.op("tensor", lambda E, pt=pt, mp=mp, half=half: E.matmul(
                                    psD[:, 0:512], ones2d[:, :],
                                    pt[:, half * 512:half * 512 + 512],
                                    start=(mp == 0 and half == 0),
                                    stop=(mp == 15 and half == 1), skip_group_check=True),
                                     reads=[ones2d, pt], writes=[psD], tag="smm")
                        p.op("vector", lambda E: E.reciprocal(bcst[:, :], psD[:, 0:512]),
                             reads=[psD], writes=[bcst], tag="recip")
                        p.op("vector", lambda E, h=h: E.tensor_tensor(
                            O_st[:, h * 512:(h + 1) * 512], psC[:, 0:512], bcst[:, :],
                            ALU.mult),
                             reads=[psC, bcst], writes=[O_st], tag="oscale")

                    if KDBG == q:
                        p.op("sync", lambda E: E.dma_start(dbg_q[:, :], Q_all[:, :]),
                             reads=[Q_all], writes=[("dbgq",)], dma=True)
                        p.op("sync", lambda E: E.dma_start(dbg_o[:, :], O_st[:, :]),
                             reads=[O_st], writes=[("dbgo",)], dma=True)
                        p.op("sync", lambda E: E.dma_start(dbg_k[:, :], K_fm[:, :]),
                             reads=[K_fm], writes=[("dbgk",)], dma=True)
                        p.op("sync", lambda E: E.dma_start(dbg_v[:, :], Vt[:, :]),
                             reads=[Vt], writes=[("dbgv",)], dma=True)
                        p.op("sync", lambda E: E.dma_start(dbg_c[:, :], negc[:, :]),
                             reads=[negc], writes=[("dbgc",)], dma=True)
                        p.op("sync", lambda E: E.dma_start(dbg_p[:, :], PTb[1][:, :]),
                             reads=[PTb[1]], writes=[("dbgp",)], dma=True)
                    for nsub in range(4):
                        t = q * 4 + nsub
                        xt = xb[t % 5]
                        psY = psA if nsub % 2 == 0 else psB
                        for dsub in range(2):
                            for h in range(NH):
                                p.op("tensor", lambda E, psY=psY, h=h, nsub=nsub, dsub=dsub: E.matmul(
                                    psY[:, dsub * 512:dsub * 512 + 512],
                                    O_st[:, h * 512 + nsub * 128:h * 512 + nsub * 128 + 128],
                                    poT[:, h * 1024 + dsub * 512:h * 1024 + dsub * 512 + 512],
                                    start=(h == 0), stop=(h == NH - 1), skip_group_check=True),
                                     reads=[O_st, poT], writes=[psY], tag="Ymm")
                        for dsub in range(2):
                            p.op("vector", lambda E, psY=psY, xt=xt, dsub=dsub: E.tensor_tensor(
                                xt[:, dsub * 512:dsub * 512 + 512],
                                psY[:, dsub * 512:dsub * 512 + 512],
                                xt[:, dsub * 512:dsub * 512 + 512], ALU.add),
                                 reads=[psY, xt], writes=[xt], tag="resid")
                        p.op("sync", lambda E, t=t, xt=xt: E.dma_start(
                            x_out["A"][t * 128:(t + 1) * 128, :], xt[:, :]),
                             reads=[xt], writes=[(KOUT["A"], t)], dma=True, tag="stxA")
                    if preB is not None:
                        for hc in range(q * 3, q * 3 + 3):
                            prep_wchunk(dwg, wgT_g, hc)
                            prep_wchunk(dwu, wuT_g, hc)
            p.barrier()

        def ffn_stage(st, pre=None):
            xin, xout = x_in[st], x_out[st]
            kin, kout = KIN[st], KOUT[st]
            with ExitStack() as bw:
                TB = lambda *a: bw.enter_context(nc.sbuf_tensor(*a))
                if pre is None:
                    wgT = TB(f"wgT{st}", [128, 8 * HID], BF16)
                    wuT = TB(f"wuT{st}", [128, 8 * HID], BF16)
                    wld = [TB(f"wld{st}{i}", [128, 1024], BF16) for i in range(2)]
                else:
                    wgT, wuT, wld = pre
                wdT = TB(f"wdT{st}", [128, 24 * D], BF16)
                lnq = TB(f"lnq{st}", [128, 8 * 512], BF16)
                hact = TB(f"hact{st}", [128, 24 * 512], BF16)
                gb = TB(f"gb{st}", [128, 512], BF16)
                ub = TB(f"ub{st}", [128, 512], BF16)

                for t in range(NT if _on("bstats") else 0):
                    xt = xb[t % 3]
                    p.op("sync", lambda E, t=t, xt=xt: E.dma_start(
                        xt[:, :], xin[t * 128:(t + 1) * 128, :]),
                         reads=[(kin, t)], writes=[xt], dma=True, tag="ldxS")
                    ln_stats(xt, t)

                if st == "B" and _on("bwprep"):
                    for dc in range(8):
                        for hg in range(3):
                            w = wld[(dc * 3 + hg) % 2]
                            p.op("gpsimd", lambda E, w=w, dc=dc, hg=hg: E.dma_start(
                                w[:, :], dwd[dc * 128:(dc + 1) * 128,
                                             hg * 1024:(hg + 1) * 1024]),
                                 writes=[w], dma=True, tag="ldwd")
                            for j in range(8):
                                p.op("tensor", lambda E, w=w, j=j: E.transpose(
                                    psT[:, j * 128:(j + 1) * 128],
                                    w[:, j * 128:(j + 1) * 128], ident[:, :]),
                                     reads=[w, ident], writes=[psT], tag="wdtr")
                            p.op("vector", lambda E, dc=dc, hg=hg: E.tensor_copy(
                                bass.AP(wdT, (hg * 8) * 1024 + dc * 128,
                                        [[24 * D, 128], [1024, 8], [1, 128]]),
                                psT[:, :]),
                                 reads=[psT], writes=[wdT], tag="wdev")
                elif st == "D":
                    p.op("sync", lambda E: E.dma_start(wgT[:, :], dwgt[:, :]),
                         reads=[("wgt_s",)], writes=[wgT], dma=True, tag="ldwgT")
                    p.op("sync", lambda E: E.dma_start(wuT[:, :], dwut[:, :]),
                         reads=[("wut_s",)], writes=[wuT], dma=True, tag="ldwuT")
                    p.op("sync", lambda E: E.dma_start(wdT[:, :], dwdt[:, :]),
                         reads=[("wdt_s",)], writes=[wdT], dma=True, tag="ldwdT")

                for qb in range(NQST if _on("bqloop") else 0):
                    for tt in range(4):
                        t = qb * 4 + tt
                        xt = xb[t % 5]
                        p.op("sync", lambda E, t=t, xt=xt: E.dma_start(
                            xt[:, :], xin[t * 128:(t + 1) * 128, :]),
                             reads=[(kin, t)], writes=[xt], dma=True, tag="ldxF")
                        lt = ln_tm[0]
                        ln_apply(xt, t, lt)
                        transpose8(lt)
                        p.op("vector", lambda E, tt=tt: E.tensor_copy(
                            bass.AP(lnq, tt * 128, [[8 * 512, 128], [512, 8], [1, 128]]),
                            psT[:, :]),
                             reads=[psT], writes=[lnq], tag="evac_lnq")
                    for hc in range(24):
                        ps = psA if hc % 2 == 0 else psB
                        for c in range(8):
                            p.op("tensor", lambda E, ps=ps, c=c, hc=hc: E.matmul(
                                ps[:, 0:512], wgT[:, c * HID + hc * 128:c * HID + hc * 128 + 128],
                                lnq[:, c * 512:(c + 1) * 512],
                                start=(c == 0), stop=(c == 7), skip_group_check=True),
                                 reads=[wgT, lnq], writes=[ps], tag="gmm")
                        for c in range(8):
                            p.op("tensor", lambda E, ps=ps, c=c, hc=hc: E.matmul(
                                ps[:, 512:1024], wuT[:, c * HID + hc * 128:c * HID + hc * 128 + 128],
                                lnq[:, c * 512:(c + 1) * 512],
                                start=(c == 0), stop=(c == 7), skip_group_check=True),
                                 reads=[wuT, lnq], writes=[ps], tag="umm")
                        p.op("scalar", lambda E, ps=ps: E.activation(
                            gb[:, :], ps[:, 0:512], AF.Silu, bias=0.0, scale=S127),
                             reads=[ps], writes=[gb], tag="silu")
                        p.op("scalar", lambda E, ps=ps: E.activation(
                            ub[:, :], ps[:, 512:1024], AF.Identity, bias=0.0, scale=S127),
                             reads=[ps], writes=[ub], tag="upev")
                        p.op("vector", lambda E, hc=hc: E.tensor_tensor(
                            hact[:, hc * 512:(hc + 1) * 512], gb[:, :], ub[:, :], ALU.mult),
                             reads=[gb, ub], writes=[hact], tag="hmul")
                    for nsub in range(4):
                        t = qb * 4 + nsub
                        xt = xb[t % 5]
                        psY = psA if nsub % 2 == 0 else psB
                        for dsub in range(2):
                            for hc in range(24):
                                p.op("tensor", lambda E, psY=psY, hc=hc, nsub=nsub, dsub=dsub: E.matmul(
                                    psY[:, dsub * 512:dsub * 512 + 512],
                                    hact[:, hc * 512 + nsub * 128:hc * 512 + nsub * 128 + 128],
                                    wdT[:, hc * 1024 + dsub * 512:hc * 1024 + dsub * 512 + 512],
                                    start=(hc == 0), stop=(hc == 23), skip_group_check=True),
                                     reads=[hact, wdT], writes=[psY], tag="dmm")
                        for dsub in range(2):
                            p.op("vector", lambda E, psY=psY, xt=xt, dsub=dsub: E.scalar_tensor_tensor(
                                xt[:, dsub * 512:dsub * 512 + 512],
                                psY[:, dsub * 512:dsub * 512 + 512], S127,
                                xt[:, dsub * 512:dsub * 512 + 512], ALU.mult, ALU.add),
                                 reads=[psY, xt], writes=[xt], tag="dresid")
                        p.op("sync", lambda E, t=t, xt=xt: E.dma_start(
                            xout[t * 128:(t + 1) * 128, :], xt[:, :]),
                             reads=[xt], writes=[(kout, t)], dma=True, tag="stxF")
                if st == "B":
                    p.op("sync", lambda E: E.dma_start(dwgt[:, :], wgT[:, :]),
                         reads=[wgT], writes=[("wgt_s",)], dma=True, tag="spillg")
                    p.op("sync", lambda E: E.dma_start(dwut[:, :], wuT[:, :]),
                         reads=[wuT], writes=[("wut_s",)], dma=True, tag="spillu")
                    p.op("sync", lambda E: E.dma_start(dwdt[:, :], wdT[:, :]),
                         reads=[wdT], writes=[("wdt_s",)], dma=True, tag="spilld")
            p.barrier()

        if "B" in stages:
            if "A" not in stages:
                for hc in range(24):
                    prep_wchunk(dwg, wgT_g, hc)
                    prep_wchunk(dwu, wuT_g, hc)
            ffn_stage("B", pre=preB)
            fscope.close()

        if "C" in stages:
            with ExitStack() as cw:
                TC = lambda *a: cw.enter_context(nc.sbuf_tensor(*a))
                WkT = TC("WkT", [128, 8 * 1024], BF16)
                WvT = TC("WvT", [128, 8 * 1024], BF16)
                WgoT = TC("WgoT", [128, 8 * 1024], BF16)

                QW = 4096
                keysb1 = TC("keysb", [128, QW], BF16)
                valsb2 = [TC(f"valsb{i}", [128, QW], BF16) for i in range(2)]
                knb2 = [TC(f"knb{i}", [128, QW], BF16) for i in range(2)]
                bb162 = [TC(f"bb16{i}", [128, QW], BF16) for i in range(2)]
                zb1 = TC("zb", [128, QW], BF16)
                m1_2 = [TC(f"m1_{i}", [128, 1024], BF16) for i in range(2)]
                m2_2 = [TC(f"m2_{i}", [128, 1024], BF16) for i in range(2)]
                m3_1 = TC("m3", [128, 1024], BF16)
                m1g2 = [TC(f"m1g{i}", [128, 1024], BF16) for i in range(2)]
                m2g2 = [TC(f"m2g{i}", [128, 1024], BF16) for i in range(2)]
                wldc = [m1g2[0], m1g2[1]]
                wperm = m2g2[0]
                lnC1 = TC("lnC", [128, 1024], BF16)
                bnbufC = [TC(f"bnbufC{i}", [128, 96], F32) for i in range(2)]
                lnvC = [TC(f"lnvC{i}", [128, 8], F32) for i in range(2)]
                crun = TC("crun", [1, 1024], F32)
                csnapAb = [TC(f"csnapAb{i}", [128, 1024], BF16) for i in range(2)]
                csnapBb = [TC(f"csnapBb{i}", [128, 1024], BF16) for i in range(2)]
                lfC2 = [TC(f"lfC{i}", [128, 1024], BF16) for i in range(2)]
                lfZ1 = TC("lfZ", [128, 1024], BF16)
                bkp = TC("bkp", [1, 1024], BF16)
                bvp = TC("bvp", [1, 1024], BF16)
                gpm = TC("gpm", [1, 1024], BF16)
                bpm = TC("bpm", [1, 1024], BF16)
                borow = TC("borow", [1, 1024], F32)
                biasCb = TC("biasCb", [1, 1024], BF16)
                gcol = TC("gcol", [128, 1], F32)
                onesC = TC("onesC", [128, 128], BF16)
                r1p = TC("r1p", [128, NT], F32)
                xbC = xb + [TC(f"xbC{i}", [128, D], F32) for i in range(3)]
                bcol = TC("bcol", [128, 1], BF16)

                def perm_row(dsrc, dst, kd):
                    p.op("sync", lambda E, dsrc=dsrc: E.dma_start(
                        crun[:, :], dsrc.rearrange("(a d) -> a d", a=1)),
                         writes=[crun], dma=True, tag="ldrow")
                    for c in range(4):
                        p.op("vector", lambda E, c=c, dst=dst: E.tensor_copy(
                            dst[:, c * 256:(c + 1) * 256],
                            bass.AP(crun, c, [[1024, 1], [4, 256]])),
                             reads=[crun], writes=[dst], tag="permrow")

                p.op("gpsimd", lambda E: E.memset(onesC[:, :], 1.0), writes=[onesC])
                perm_row(dbk, bkp, "bk")
                perm_row(dbv, bvp, "bv")
                perm_row(dlng, gpm, "lng")
                perm_row(dlnb, bpm, "lnb")
                p.op("sync", lambda E: E.dma_start(
                    borow[:, :], dbo.rearrange("(a d) -> a d", a=1)),
                     writes=[borow], dma=True, tag="ldbo")

                for nm, dsrc, dst in (("k", dWk, WkT), ("v", dWv, WvT)):
                    rsrc = dsrc.rearrange("(hh p four) i -> four hh p i", hh=2, p=128, four=4)
                    for q in range(8):
                        w = wldc[q % 2]
                        fo, hh = q // 2, q % 2
                        p.op("gpsimd", lambda E, w=w, rsrc=rsrc, fo=fo, hh=hh: E.dma_start(
                            w[:, :], rsrc[fo, hh, :, :]),
                             writes=[w], dma=True, tag="ldWkv")
                        for ci in range(8):
                            p.op("tensor", lambda E, w=w, ci=ci: E.transpose(
                                psT[:, ci * 128:(ci + 1) * 128],
                                w[:, ci * 128:(ci + 1) * 128], ident[:, :]),
                                 reads=[w, ident], writes=[psT], tag="wkvtr")
                        p.op("vector", lambda E, dst=dst, q=q: E.tensor_copy(
                            bass.AP(dst, q * 128, [[8 * 1024, 128], [1024, 8], [1, 128]]),
                            psT[:, :]),
                             reads=[psT], writes=[dst], tag="wkvev")

                for co in range(8):
                    w = wldc[co % 2]
                    p.op("gpsimd", lambda E, w=w, co=co: E.dma_start(
                        w[:, :], dWo[co * 128:(co + 1) * 128, :]),
                         writes=[w], dma=True, tag="ldWo")
                    for c in range(4):
                        p.op("vector", lambda E, w=w, c=c: E.tensor_copy(
                            wperm[:, c * 256:(c + 1) * 256],
                            bass.AP(w, c, [[1024, 128], [4, 256]])),
                             reads=[w], writes=[wperm], tag="woperm")
                    for ci in range(8):
                        p.op("tensor", lambda E, ci=ci: E.transpose(
                            psT[:, ci * 128:(ci + 1) * 128],
                            wperm[:, ci * 128:(ci + 1) * 128], ident[:, :]),
                             reads=[wperm, ident], writes=[psT], tag="wotr")
                    p.op("vector", lambda E, co=co: E.tensor_copy(
                        bass.AP(WgoT, co * 128, [[8 * 1024, 128], [1024, 8], [1, 128]]),
                        psT[:, :]),
                         reads=[psT], writes=[WgoT], tag="woev")

                for dsub in range(2):
                    for ci in range(8):
                        p.op("tensor", lambda E, ci=ci: E.matmul(
                            psD[:, 0:1], bpm[0:1, ci * 128:(ci + 1) * 128],
                            ones_row[0:1, 0:1], start=True, stop=True),
                             reads=[bpm, ones_row], writes=[psD], tag="bcolmm")
                        p.op("vector", lambda E: E.tensor_copy(bcol[:, :], psD[:, 0:1]),
                             reads=[psD], writes=[bcol], tag="bcolcp")
                        p.op("tensor", lambda E, ci=ci, dsub=dsub: E.matmul(
                            psE[0:1, 0:512], bcol[:, :],
                            WgoT[:, ci * 1024 + dsub * 512:ci * 1024 + dsub * 512 + 512],
                            start=(ci == 0), stop=(ci == 7), skip_group_check=True),
                             reads=[bcol, WgoT], writes=[psE], tag="biasmm")
                    p.op("vector", lambda E, dsub=dsub: E.tensor_tensor(
                        biasCb[0:1, dsub * 512:dsub * 512 + 512], psE[0:1, 0:512],
                        borow[0:1, dsub * 512:dsub * 512 + 512], ALU.add),
                         reads=[psE, borow], writes=[biasCb], tag="biasadd")

                for ci in range(8):
                    p.op("tensor", lambda E, ci=ci: E.matmul(
                        psD[:, 0:1], gpm[0:1, ci * 128:(ci + 1) * 128],
                        ones_row[0:1, 0:1], start=True, stop=True),
                         reads=[gpm, ones_row], writes=[psD], tag="gcolmm")
                    p.op("vector", lambda E: E.tensor_copy(gcol[:, :], psD[:, 0:1]),
                         reads=[psD], writes=[gcol], tag="gcolcp")
                    p.op("scalar", lambda E, ci=ci: E.activation(
                        WgoT[:, ci * 1024:(ci + 1) * 1024],
                        WgoT[:, ci * 1024:(ci + 1) * 1024],
                        AF.Identity, bias=0.0, scale=gcol[:, :]),
                         reads=[WgoT, gcol], writes=[WgoT], tag="gscale")


                def plane(buf, comp):
                    return bass.AP(buf, comp * 256, [[2048, 128], [1024, 2], [1, 256]])

                QUAT_BOUND = [
                    (0, [(1, 0, 0), (-1, 1, 1), (-1, 2, 2), (-1, 3, 3)]),
                    (1, [(1, 0, 1), (1, 1, 0), (1, 2, 3), (-1, 3, 2)]),
                    (2, [(1, 0, 2), (-1, 1, 3), (1, 2, 0), (1, 3, 1)]),
                    (3, [(1, 0, 3), (1, 1, 2), (-1, 2, 1), (1, 3, 0)]),
                ]
                QUAT_RETR = [
                    (0, [(1, 0, 0), (1, 1, 1), (1, 2, 2), (1, 3, 3)]),
                    (1, [(1, 1, 0), (-1, 0, 1), (-1, 2, 3), (1, 3, 2)]),
                    (2, [(1, 1, 3), (-1, 0, 2), (1, 2, 0), (-1, 3, 1)]),
                    (3, [(1, 2, 1), (-1, 0, 3), (-1, 1, 2), (1, 3, 0)]),
                ]

                def quat(table, a_buf, b_buf, out_buf, m1, m2, m1g, m2g):
                    for oc, terms in table:
                        eng = "vector" if oc < 3 else "gpsimd"
                        s1, s2 = (m1, m2) if oc < 3 else (m1g, m2g)
                        for i, (sg, ac, bc) in enumerate(terms):
                            tgt = s2 if i > 0 else s1
                            p.op(eng, lambda E, ac=ac, bc=bc, tgt=tgt,
                                 a_buf=a_buf, b_buf=b_buf: E.tensor_tensor(
                                     tgt[:, :], plane4(a_buf, ac), plane4(b_buf, bc),
                                     ALU.mult),
                                 reads=[a_buf, b_buf], writes=[tgt], tag="qmul")
                            if i > 0:
                                opx = ALU.add if sg > 0 else ALU.subtract
                                if i < 3:
                                    p.op(eng, lambda E, opx=opx, s1=s1, s2=s2: E.tensor_tensor(
                                        s1[:, :], s1[:, :], s2[:, :], opx),
                                         reads=[s1, s2], writes=[s1], tag="qacc")
                                else:
                                    p.op(eng, lambda E, opx=opx, oc=oc, s1=s1, s2=s2,
                                         out_buf=out_buf: E.tensor_tensor(
                                             plane4(out_buf, oc), s1[:, :], s2[:, :], opx),
                                         reads=[s1, s2], writes=[out_buf], tag="qout")

                p.op("vector", lambda E: E.memset(crun[:, :], 0.0), writes=[crun])
                p.op("vector", lambda E: E.memset(csnapAb[0][0:1, :], 0.0),
                     writes=[(id(csnapAb[0]), 0)])

                def cbrow(pb, u):
                    P = pb % 2
                    if u < 3:
                        return csnapAb[P], 32 * u
                    return csnapBb[P], 0

                def plane4(buf, comp):
                    return buf[:, comp * 1024:(comp + 1) * 1024]

                def tslice(buf, u, c0, nc_):
                    return bass.AP(buf, c0 * 1024 + u * 256,
                                   [[QW, 128], [1024, nc_], [1, 256]])

                def front(pb):
                    par = pb % 2
                    keysb, valsb, knb = keysb1, valsb2[par], knb2[par]
                    bb16 = bb162[par]
                    m1, m2, m3 = m1_2[par], m2_2[par], m3_1
                    m1g, m2g = m1g2[par], m2g2[par]
                    for u in range(4):
                        t = 4 * pb + u
                        xt = xbC[t % 8]
                        p.op("sync", lambda E, t=t, xt=xt: E.dma_start(
                            xt[:, :], x_in["C"][t * 128:(t + 1) * 128, :]),
                             reads=[(KIN["C"], t)], writes=[xt], dma=True, tag="ldxC")
                        ln_stats(xt, t)
                        lt = lnC1
                        ln_apply(xt, t, lt)
                        transpose8(lt)
                        lf = lfC2[par]
                        p.op("scalar", lambda E, lf=lf: E.activation(
                            lf[:, :], psT[:, :], AF.Identity, bias=0.0, scale=1.0),
                             reads=[psT], writes=[lf], tag="evac_lnfmC")
                        for dst_ps, wt, brow, obuf in ((psA, WkT, bkp, keysb),
                                                       (psB, WvT, bvp, valsb)):
                            for ds in range(2):
                                for c in range(8):
                                    p.op("tensor", lambda E, dst_ps=dst_ps, wt=wt, c=c, ds=ds,
                                         lf=lf: E.matmul(
                                             dst_ps[:, ds * 512:ds * 512 + 512],
                                             lf[:, c * 128:(c + 1) * 128],
                                             wt[:, c * 1024 + ds * 512:c * 1024 + ds * 512 + 512],
                                             start=(c == 0), stop=False, skip_group_check=True),
                                         reads=[lf, wt], writes=[dst_ps], tag="kvmm")
                                p.op("tensor", lambda E, dst_ps=dst_ps, brow=brow, ds=ds: E.matmul(
                                    dst_ps[:, ds * 512:ds * 512 + 512], ones_row[:, :],
                                    brow[0:1, ds * 512:ds * 512 + 512],
                                    start=False, stop=True, skip_group_check=True),
                                     reads=[ones_row, brow], writes=[dst_ps], tag="kvbias")
                            p.op("scalar", lambda E, obuf=obuf, u=u, dst_ps=dst_ps: E.activation(
                                tslice(obuf, u, 0, 4), dst_ps[:, :],
                                AF.Identity, bias=0.0, scale=1.0),
                                 reads=[dst_ps], writes=[obuf], tag="kvev")
                    t0 = 4 * pb
                    p.op("vector", lambda E, t0=t0: E.tensor_scalar(
                        r1p[:, t0:t0 + 4], rstd_all[:, t0:t0 + 4], 1.0, None, ALU.add),
                         reads=[("rstd", t0 + k) for k in range(4)],
                         writes=[("r1p", t0 + k) for k in range(4)], tag="r1p")
                    p.op("vector", lambda E, keysb=keysb, m3=m3: E.tensor_tensor(
                        m3[:, :], plane4(keysb, 0), plane4(keysb, 0), ALU.mult),
                         reads=[keysb], writes=[m3], tag="ss0")
                    for comp in range(1, 4):
                        p.op("vector", lambda E, keysb=keysb, m2=m2, comp=comp: E.tensor_tensor(
                            m2[:, :], plane4(keysb, comp), plane4(keysb, comp), ALU.mult),
                             reads=[keysb], writes=[m2], tag="ssq")
                        p.op("vector", lambda E, m2=m2, m3=m3: E.tensor_tensor(
                            m3[:, :], m3[:, :], m2[:, :], ALU.add),
                             reads=[m2, m3], writes=[m3], tag="ssacc")
                    p.op("vector", lambda E, m3=m3: E.tensor_scalar(
                        m3[:, :], m3[:, :], 1e-18, None, ALU.max),
                         reads=[m3], writes=[m3], tag="ssmax")
                    p.op("scalar", lambda E, m3=m3: E.activation(
                        m3[:, :], m3[:, :], AF.Ln, bias=0.0, scale=1.0),
                         reads=[m3], writes=[m3], tag="ssln")
                    p.op("scalar", lambda E, m3=m3: E.activation(
                        m3[:, :], m3[:, :], AF.Exp, bias=0.0, scale=-0.5),
                         reads=[m3], writes=[m3], tag="ssexp")
                    for comp in range(4):
                        p.op("vector", lambda E, comp=comp, m3=m3, knb=knb,
                             keysb=keysb: E.tensor_tensor(
                                 plane4(knb, comp), plane4(keysb, comp), m3[:, :], ALU.mult),
                             reads=[keysb, m3], writes=[knb], tag="knmul")
                    quat(QUAT_BOUND, valsb, knb, bb16, m1, m2, m1g, m2g)
                    for u in range(4):
                        for ds in range(2):
                            po = 0 if ds == 0 else 64
                            p.op("tensor", lambda E, u=u, ds=ds, po=po, bb16=bb16: E.matmul(
                                psE[po:po + 1, 0:512], ones_col[:, :],
                                tslice(bb16, u, 2 * ds, 2),
                                start=True, stop=True, skip_group_check=True),
                                 reads=[ones_col, bb16], writes=[psE], tag="Smm")
                        if u < 3:
                            db_, dbo_ = cbrow(pb, u + 1)
                        else:
                            db_, dbo_ = csnapAb[(pb + 1) % 2], 0
                        for ds in range(2):
                            po = 0 if ds == 0 else 64
                            p.op("vector", lambda E, ds=ds, po=po: E.tensor_tensor(
                                crun[0:1, ds * 512:ds * 512 + 512],
                                crun[0:1, ds * 512:ds * 512 + 512],
                                psE[po:po + 1, 0:512], ALU.add),
                                 reads=[crun, psE], writes=[crun], tag="carryupd")
                        p.op("vector", lambda E, db_=db_, dbo_=dbo_: E.tensor_copy(
                            db_[dbo_:dbo_ + 1, :], crun[:, :]),
                             reads=[crun], writes=[(id(db_), dbo_)], tag="carrybf")

                def back(pb):
                    par = pb % 2
                    valsb, knb = valsb2[par], knb2[par]
                    bb16, zb = bb162[par], zb1
                    m1, m2, m1g, m2g = m1_2[par], m2_2[par], m1g2[par], m2g2[par]
                    for u in range(4):
                        cb_, cbo = cbrow(pb, u)
                        for ds in range(2):
                            psM = psC if ds == 0 else psD
                            p.op("tensor", lambda E, psM=psM, u=u, ds=ds, bb16=bb16: E.matmul(
                                psM[:, 0:512], triu[:, :],
                                tslice(bb16, u, 2 * ds, 2),
                                start=True, stop=False, skip_group_check=True),
                                 reads=[triu, bb16], writes=[psM], tag="trilmm")
                            p.op("tensor", lambda E, psM=psM, ds=ds, cb_=cb_, cbo=cbo: E.matmul(
                                psM[:, 0:512], onesC[cbo:cbo + 1, :],
                                cb_[cbo:cbo + 1, ds * 512:ds * 512 + 512],
                                start=False, stop=True, skip_group_check=True),
                                 reads=[onesC, (id(cb_), cbo)], writes=[psM], tag="carrymm")
                        t = 4 * pb + u
                        for ds in range(2):
                            psM = psC if ds == 0 else psD
                            p.op("scalar", lambda E, psM=psM, u=u, ds=ds, t=t,
                                 valsb=valsb: E.activation(
                                     tslice(valsb, u, 2 * ds, 2),
                                     psM[:, 0:512], AF.Identity,
                                     bias=0.0, scale=rpos[:, t:t + 1]),
                                 reads=[psM, rpos], writes=[valsb], tag="Mevac")
                    quat(QUAT_RETR, valsb, knb, bb16, m1, m2, m1g, m2g)
                    retrb = bb16
                    sb, sv = bnbufC[par], lnvC[par]
                    for u in range(4):
                        for cc in range(4):
                            p.op("vector", lambda E, u=u, cc=cc, sb=sb, retrb=retrb: E.bn_stats(
                                sb[:, u * 24 + cc * 6:u * 24 + cc * 6 + 6],
                                retrb[:, cc * 1024 + u * 256:cc * 1024 + u * 256 + 256]),
                                 reads=[retrb], writes=[(id(sb), u)], tag="bnsr")
                        p.op("vector", lambda E, u=u, sb=sb, sv=sv: E.bn_aggr(
                            sv[:, 2 * u:2 * u + 2], sb[:, u * 24:u * 24 + 24]),
                             reads=[(id(sb), u)], writes=[(id(sv), u)], tag="bnar")
                    j0 = NT + 4 * pb
                    svkeys = [(id(sv), k) for k in range(4)]
                    rkeys = [("rstd", j0 + k) for k in range(4)]
                    p.op("scalar", lambda E, j0=j0, sv=sv: E.activation(
                        rstd_all[:, j0:j0 + 4], bass.AP(sv, 1, [[8, 128], [2, 4]]),
                        AF.Ln, bias=ceps, scale=1.0),
                         reads=svkeys + [ckey(EPS)], writes=rkeys, tag="rstdC1")
                    p.op("scalar", lambda E, j0=j0: E.activation(
                        rstd_all[:, j0:j0 + 4], rstd_all[:, j0:j0 + 4], AF.Exp,
                        bias=0.0, scale=-0.5),
                         reads=rkeys, writes=rkeys, tag="rstdC2")
                    p.op("vector", lambda E, j0=j0, sv=sv: E.scalar_tensor_tensor(
                        nmr_all[:, j0:j0 + 4], bass.AP(sv, 0, [[8, 128], [2, 4]]), -1.0,
                        rstd_all[:, j0:j0 + 4], ALU.mult, ALU.mult),
                         reads=svkeys + rkeys,
                         writes=[("nmr", j0 + k) for k in range(4)], tag="nmrC")
                    for u in range(4):
                        t = 4 * pb + u
                        xt = xbC[t % 8]
                        j2 = NT + t
                        p.op("scalar", lambda E, u=u, j2=j2, zb=zb,
                             retrb=retrb: E.activation(
                                 zb[:, u * 1024:(u + 1) * 1024], tslice(retrb, u, 0, 4),
                                 AF.Identity,
                                 bias=nmr_all[:, j2:j2 + 1], scale=rstd_all[:, j2:j2 + 1]),
                             reads=[retrb, ("rstd", j2), ("nmr", j2)], writes=[zb], tag="zapply")
                        for c in range(8):
                            p.op("tensor", lambda E, c=c, u=u, zb=zb: E.transpose(
                                psT[:, c * 128:(c + 1) * 128],
                                zb[:, u * 1024 + c * 128:u * 1024 + c * 128 + 128],
                                ident[:, :]),
                                 reads=[zb, ident], writes=[psT], tag="ztr")
                        lf = lfZ1
                        p.op("scalar", lambda E, lf=lf: E.activation(
                            lf[:, :], psT[:, :], AF.Identity, bias=0.0, scale=1.0),
                             reads=[psT], writes=[lf], tag="zfmev")
                        for ds in range(2):
                            psO = psC if ds == 0 else psD
                            for c in range(8):
                                p.op("tensor", lambda E, psO=psO, c=c, ds=ds, lf=lf: E.matmul(
                                    psO[:, 0:512],
                                    lf[:, c * 128:(c + 1) * 128],
                                    WgoT[:, c * 1024 + ds * 512:c * 1024 + ds * 512 + 512],
                                    start=(c == 0), stop=False, skip_group_check=True),
                                     reads=[lf, WgoT], writes=[psO], tag="gomm")
                            p.op("tensor", lambda E, psO=psO, ds=ds: E.matmul(
                                psO[:, 0:512], ones_row[:, :],
                                biasCb[0:1, ds * 512:ds * 512 + 512],
                                start=False, stop=True, skip_group_check=True),
                                 reads=[ones_row, biasCb], writes=[psO], tag="gobias")
                        p.op("vector", lambda E, xt=xt, t=t: E.tensor_scalar(
                            xt[:, :], xt[:, :], r1p[:, t:t + 1], nmr_all[:, t:t + 1],
                            ALU.mult, ALU.add),
                             reads=[xt, ("r1p", t), ("nmr", t)], writes=[xt], tag="residC0")
                        for ds in range(2):
                            psO = psC if ds == 0 else psD
                            p.op("vector", lambda E, psO=psO, xt=xt, ds=ds: E.tensor_tensor(
                                xt[:, ds * 512:ds * 512 + 512],
                                psO[:, 0:512],
                                xt[:, ds * 512:ds * 512 + 512], ALU.add),
                                 reads=[psO, xt], writes=[xt], tag="residC1")
                        p.op("sync", lambda E, t=t, xt=xt: E.dma_start(
                            x_out["C"][t * 128:(t + 1) * 128, :], xt[:, :]),
                             reads=[xt], writes=[(KOUT["C"], t)], dma=True, tag="stxC")

                front(0)
                for pb in range(8):
                    if pb + 1 < 8:
                        front(pb + 1)
                    back(pb)
            p.barrier()

        if "D" in stages:
            ffn_stage("D")

        p.barrier()
        p.emit(block, sems)
    return nc


def _shard_inputs(inputs):
    per_core = []
    x = np.ascontiguousarray(inputs["L0_attn_a_x"])
    shared = {
        "pq": np.ascontiguousarray(inputs["L0_attn_a_proj_q"]),
        "pk": np.ascontiguousarray(inputs["L0_attn_a_proj_k"]),
        "pv": np.ascontiguousarray(inputs["L0_attn_a_proj_v"]),
        "po": np.ascontiguousarray(inputs["L0_attn_a_proj_o"]),
        "Wk": np.ascontiguousarray(inputs["L0_attn_b_Wk"]),
        "bk": np.ascontiguousarray(inputs["L0_attn_b_bk"]),
        "Wv": np.ascontiguousarray(inputs["L0_attn_b_Wv"]),
        "bv": np.ascontiguousarray(inputs["L0_attn_b_bv"]),
        "lng": np.ascontiguousarray(inputs["L0_attn_b_ln_g"]),
        "lnb": np.ascontiguousarray(inputs["L0_attn_b_ln_b"]),
        "Wo": np.ascontiguousarray(inputs["L0_attn_b_Wo"]),
        "bo": np.ascontiguousarray(inputs["L0_attn_b_bo"]),
        "wg": np.ascontiguousarray(inputs["L0_ffn_w_gate"]),
        "wu": np.ascontiguousarray(inputs["L0_ffn_w_up"]),
        "wd": np.ascontiguousarray(inputs["L0_ffn_w_down"]),
    }
    for b in range(8):
        m = {"x0": np.ascontiguousarray(x[b])}
        m.update(shared)
        per_core.append(m)
    return per_core


def kernel(**inputs):
    nc = build()
    in_maps = _shard_inputs(inputs)
    res = run_bass_kernel_spmd(nc, in_maps, core_ids=list(range(8)))
    return np.stack([r["y"] for r in res.results], axis=0)
